# version 01
# baseline (speedup 1.0000x reference)
import contextlib
import numpy as np
import ml_dtypes
import concourse.bass as bass
import concourse.mybir as mybir
from concourse.bass_utils import run_bass_kernel_spmd

F32 = mybir.dt.float32
BF16 = mybir.dt.bfloat16
AF = mybir.ActivationFunctionType
ALU = mybir.AluOpType

NB = 4
NT = 4096
ND = 1024
NCORES = 8
HALF = NT // 2

ENGINES = ("tensor", "scalar", "vector", "gpsimd", "sync")
SEMS = ("pe", "act", "dve", "gpe", "gpd", "syd")


class Ctx:

    def __init__(self, nc, es):
        self.nc = nc
        self.es = es
        self.ops = {e: [] for e in ENGINES}
        self.sems = {s: es.enter_context(nc.semaphore(s)) for s in SEMS}
        self.cnt = {s: 0 for s in SEMS}
        self.paranoid = False

    def on(self, eng, fn):
        self.ops[eng].append(fn)

    def wait(self, eng, sem, val):
        if val <= 0:
            return
        h = self.sems[sem]
        self.ops[eng].append(lambda q: q.wait_ge(h, val))

    def wait_marks(self, eng, marks, only=None):
        for s, v in marks.items():
            if v > 0 and (only is None or s in only):
                self.wait(eng, s, v)

    def mark(self):
        return dict(self.cnt)

    def inc(self, eng, sem, fn, amt=1):
        h = self.sems[sem]
        self.ops[eng].append(lambda q: fn(q).then_inc(h, amt))
        self.cnt[sem] += amt
        if self.paranoid:
            m = dict(self.cnt)
            for e in ENGINES:
                for s_, v_ in m.items():
                    if v_ > 0:
                        hh = self.sems[s_]
                        self.ops[e].append(
                            lambda q, hh=hh, v_=v_: q.wait_ge(hh, v_))

    def sinc(self, eng, sem, fn, amt=1):
        self.inc(eng, sem, fn, amt)
        self.wait(eng, sem, self.cnt[sem])

    def dma(self, eng, out, in_, **kw):
        sem = "syd" if eng == "sync" else "gpd"
        self.inc(eng, sem, lambda q: q.dma_start(out=out, in_=in_, **kw), 16)

    def emit(self):
        nc = self.nc
        with nc.Block() as block:
            for eng in ENGINES:
                items = self.ops[eng]
                if not items:
                    continue

                def make(eng_items):
                    def body(q):
                        for it in eng_items:
                            it(q)
                    return body
                getattr(block, eng)(make(list(items)))

    def sb(self, name, shape, dtype):
        return self.es.enter_context(self.nc.sbuf_tensor(name, shape, dtype))

    def ps(self, name, nbanks=1):
        return self.es.enter_context(
            self.nc.psum_tensor(name, [128, 512 * nbanks], F32))


def barrier_all(ctx, engines):
    m = ctx.mark()
    for e in engines:
        ctx.wait_marks(e, m)


def build_ln(R):
    nc = bass.Bass("TRN2", target_bir_lowering=False, debug=False, detect_race_conditions=False)
    es = contextlib.ExitStack()
    x = nc.dram_tensor("x", [R, ND], F32, kind="ExternalInput").ap()
    xn = nc.dram_tensor("xn", [R, ND], F32, kind="ExternalOutput").ap()
    ntile = R // 128
    with es:
        c = Ctx(nc, es)
        xb = [c.sb(f"xb{i}", [128, ND], F32) for i in range(2)]
        ob = [c.sb(f"ob{i}", [128, ND], F32) for i in range(2)]
        scr = c.sb("scr", [128, ND], F32)
        s1 = c.sb("s1", [128, 1], F32)
        s2 = c.sb("s2", [128, 1], F32)
        mu = c.sb("mu", [128, 1], F32)
        var = c.sb("var", [128, 1], F32)
        rstd = c.sb("rstd", [128, 1], F32)

        for i in range(ntile):
            p = i % 2
            if i >= 2:
                c.wait("sync", "dve", 7 * (i - 1))
            c.dma("sync", xb[p][:, :], x[128 * i:128 * (i + 1), :])
            m = c.mark()
            c.wait("vector", "syd", m["syd"])
            c.inc("vector", "dve",
                  lambda q, p=p: q.reduce_sum(s1[:, 0:1], xb[p][:, :],
                                              axis=mybir.AxisListType.X))
            c.wait("scalar", "syd", m["syd"])
            if i > 0:
                c.wait("scalar", "dve", 7 * i)
            c.inc("scalar", "act",
                  lambda q, p=p: q.activation(scr[:, :], xb[p][:, :], AF.Square,
                                              accum_out=s2[:, 0:1]))
            m2 = c.mark()
            c.wait("vector", "act", m2["act"])

            c.sinc("vector", "dve",
                   lambda q: q.tensor_scalar(mu[:, 0:1], s1[:, 0:1], 1.0 / ND,
                                             None, ALU.mult))
            c.sinc("vector", "dve",
                   lambda q: q.tensor_scalar(var[:, 0:1], s2[:, 0:1], 1.0 / ND,
                                             None, ALU.mult))
            c.sinc("vector", "dve",
                   lambda q: q.tensor_tensor(scr[:, 0:1], mu[:, 0:1], mu[:, 0:1],
                                             ALU.mult))
            c.sinc("vector", "dve",
                   lambda q: q.tensor_scalar(var[:, 0:1], var[:, 0:1], 1e-5,
                                             scr[:, 0:1], ALU.add, ALU.subtract))
            c.sinc("vector", "dve",
                   lambda q: q.reciprocal(rstd[:, 0:1], var[:, 0:1]))
            m2b = c.mark()
            c.wait("scalar", "dve", m2b["dve"])
            c.inc("scalar", "act",
                  lambda q: q.activation(rstd[:, 0:1], rstd[:, 0:1], AF.Sqrt))
            m3 = c.mark()
            c.wait("vector", "act", m3["act"])
            if i >= 2:
                c.wait("vector", "syd", 32 * (i - 1))
            c.inc("vector", "dve",
                  lambda q, p=p: q.tensor_scalar(ob[p][:, :], xb[p][:, :],
                                                 mu[:, 0:1], rstd[:, 0:1],
                                                 ALU.subtract, ALU.mult))
            m4 = c.mark()
            c.wait("sync", "dve", m4["dve"])
            c.dma("sync", xn[128 * i:128 * (i + 1), :], ob[p][:, :])
        mf = c.mark()
        c.wait("sync", "syd", mf["syd"])
        c.emit()
    return nc


def build_l0_attn(Tq, Tk, qoff_max):
    nc = bass.Bass("TRN2", target_bir_lowering=False, debug=False,
                   detect_race_conditions=False)
    es = contextlib.ExitStack()
    xnT = nc.dram_tensor("xnT", [ND, Tk], F32, kind="ExternalInput").ap()
    qnT = nc.dram_tensor("qnT", [ND, Tq], F32, kind="ExternalInput").ap()
    wat = nc.dram_tensor("wat", [ND, 3 * ND], F32, kind="ExternalInput").ap()
    bat = nc.dram_tensor("bat", [3 * ND], F32, kind="ExternalInput").ap()
    wpr = nc.dram_tensor("wpr", [ND, ND], F32, kind="ExternalInput").ap()
    bpr = nc.dram_tensor("bpr", [ND], F32, kind="ExternalInput").ap()
    qoff = nc.dram_tensor("qoff", [128, 1], F32, kind="ExternalInput").ap()
    ya = nc.dram_tensor("ya", [Tq, ND], F32, kind="ExternalOutput").ap()

    NKT = Tk // 128
    SEGK = min(Tk, 512)
    SEGQ = min(Tq, 512)
    WQ = min(Tq, 128)
    NC8 = ND // 128

    with es:
        c = Ctx(nc, es)
        k_fm = c.sb("k_fm", [128, NC8 * Tk], BF16)
        v_tm = c.sb("v_tm", [128, NKT * ND], BF16)
        q_fm = c.sb("q_fm", [128, NC8 * SEGQ], BF16)
        bat_k = c.sb("bat_k", [128, NC8], F32)
        bat_q = c.sb("bat_q", [128, NC8], F32)
        bv_row = c.sb("bv_row", [1, ND], BF16)
        bpr_row = c.sb("bpr_row", [1, ND], BF16)
        ones_c = c.sb("ones_c", [128, 1], BF16)
        ones_r = c.sb("ones_r", [1, 128], BF16)
        ones_r32 = c.sb("ones_r32", [1, 128], F32)
        IJ0 = c.sb("IJ0", [128, WQ], F32)
        IJq = c.sb("IJq", [128, WQ], F32)
        qoff_sb = c.sb("qoff_sb", [128, 1], F32)
        negb = c.sb("negb", [1, SEGQ], BF16)
        nrow = c.sb("nrow", [1, max(SEGK, SEGQ)], F32)
        Mk2 = c.sb("Mk2", [1, 1], F32)
        Mk = c.sb("Mk", [1, 1], F32)
        sq_scr = c.sb("sq_scr", [128, 1024], BF16)
        av_qb = c.sb("av_qb", [128, NC8 * WQ], BF16)
        rd_sb = c.sb("rd_sb", [1, WQ], F32)
        bc_sb = c.sb("bc_sb", [128, WQ], F32)
        outb = c.sb("outb", [128, ND], F32)
        wat_k = c.sb("wat_k", [128, NC8 * ND], BF16)
        wat_v = c.sb("wat_v", [128, NC8 * ND], BF16)
        wpr_sb = wat_v
        xn_seg = c.sb("xn_seg", [128, max(NC8 * SEGK, NKT * WQ)], BF16)
        P_T = xn_seg

        ps_s = [c.ps("ps_s0"), c.ps("ps_s1")]
        ps_av = [c.ps("ps_av0"), c.ps("ps_av1")]
        ps_d = c.ps("ps_d")
        ps_n = c.ps("ps_n", 2)
        ps_o = ps_av

        def gp_init(q):
            q.memset(ones_c[:, :], 1.0)
            q.memset(ones_r[:, :], 1.0)
            q.memset(ones_r32[:, :], 1.0)
            return q.iota(IJ0[:, :], [[1, WQ]], channel_multiplier=-1,
                          allow_small_or_imprecise_dtypes=True)
        c.inc("gpsimd", "gpe", gp_init)
        c.dma("sync", qoff_sb[:, :], qoff)
        c.dma("gpsimd", bat_k[:, :], bat.rearrange("(c p) -> p c", p=128)[:, 8:16],
              allow_slow_non_contiguous=True)
        c.dma("gpsimd", bat_q[:, :], bat.rearrange("(c p) -> p c", p=128)[:, 0:8],
              allow_slow_non_contiguous=True)
        c.dma("gpsimd", bv_row[0:1, :], bat[None, 2048:3072])
        c.dma("gpsimd", bpr_row[0:1, :], bpr[None, :])
        c.dma("gpsimd", wat_k[:, :].rearrange("p (c o) -> p c o", c=8),
              wat[:, ND:2 * ND].rearrange("(c p) o -> p c o", p=128))
        c.dma("gpsimd", wat_v[:, :].rearrange("p (c o) -> p c o", c=8),
              wat[:, 2 * ND:3 * ND].rearrange("(c p) o -> p c o", p=128))
        c.inc("vector", "dve", lambda q: q.memset(Mk2[:, :], 0.0))

        def normsq_row(src, src_col0, src_stride, width):
            pe_b = c.cnt["pe"]
            nsb = (width + 511) // 512
            wlast = width - 512 * (nsb - 1)
            idx = 0
            for ch in range(NC8):
                for sb_ in range(nsb):
                    w = 512 if sb_ < nsb - 1 else wlast
                    half = idx % 2
                    if idx >= 2:
                        c.wait("scalar", "pe", pe_b + idx - 1)
                    c.inc("scalar", "act",
                          lambda q, ch=ch, sb_=sb_, half=half, w=w:
                          q.activation(sq_scr[:, half * 512:half * 512 + w],
                                       src[:, src_col0 + ch * src_stride + sb_ * 512:
                                           src_col0 + ch * src_stride + sb_ * 512 + w],
                                       AF.Square))
                    c.wait("tensor", "act", c.cnt["act"])
                    c.inc("tensor", "pe",
                          lambda q, ch=ch, sb_=sb_, half=half, w=w:
                          q.matmul(ps_n[0:1, sb_ * 512:sb_ * 512 + w],
                                   ones_c[:, 0:1],
                                   sq_scr[:, half * 512:half * 512 + w],
                                   start=(ch == 0), stop=(ch == NC8 - 1),
                                   skip_group_check=True))
                    idx += 1
        for seg in range(Tk // SEGK):
            k0 = seg * SEGK
            if seg > 0:
                c.wait_marks("gpsimd", c.mark(), only=("pe",))
            for ch in range(NC8):
                c.dma("gpsimd", xn_seg[:, ch * SEGK:(ch + 1) * SEGK],
                      xnT[128 * ch:128 * (ch + 1), k0:k0 + SEGK])
            m = c.mark()
            c.wait("tensor", "gpd", m["gpd"])
            c.wait("tensor", "dve", m["dve"])
            nb = max(1, SEGK // 512)
            wb = SEGK // nb
            pe_b, act_b = c.cnt["pe"], c.cnt["act"]
            job = 0
            for mm_ in range(NC8):
                for tb in range(nb):
                    pw = job % 2
                    if job >= 2:
                        c.wait("tensor", "act", act_b + job - 1)

                    def mms(q, mm_=mm_, tb=tb, pw=pw, wb=wb):
                        for ch in range(NC8):
                            r = q.matmul(ps_s[pw][:, 0:wb],
                                         wat_k[:, ch * ND + 128 * mm_:ch * ND + 128 * (mm_ + 1)],
                                         xn_seg[:, ch * SEGK + tb * wb:ch * SEGK + (tb + 1) * wb],
                                         start=(ch == 0), stop=(ch == NC8 - 1))
                        return r
                    c.inc("tensor", "pe", mms)
                    c.wait("scalar", "pe", pe_b + job + 1)
                    c.inc("scalar", "act",
                          lambda q, mm_=mm_, tb=tb, pw=pw, wb=wb, k0=k0:
                          q.activation(k_fm[:, mm_ * Tk + k0 + tb * wb:
                                            mm_ * Tk + k0 + (tb + 1) * wb],
                                       ps_s[pw][:, 0:wb], AF.Identity,
                                       bias=bat_k[:, mm_:mm_ + 1]))
                    job += 1
            pe_b, dve_b = c.cnt["pe"], c.cnt["dve"]
            job = 0
            for tt in range(SEGK // 128):
                for ob in range(ND // 512):
                    pw = job % 2
                    if job >= 2:
                        c.wait("tensor", "dve", dve_b + job - 1)

                    def mmsv(q, tt=tt, ob=ob, pw=pw):
                        for ch in range(NC8):
                            q.matmul(ps_av[pw][:, 0:512],
                                     xn_seg[:, ch * SEGK + tt * 128:ch * SEGK + (tt + 1) * 128],
                                     wat_v[:, ch * ND + ob * 512:ch * ND + (ob + 1) * 512],
                                     start=(ch == 0), stop=False)
                        return q.matmul(ps_av[pw][:, 0:512], ones_r[0:1, :],
                                        bv_row[0:1, ob * 512:(ob + 1) * 512],
                                        start=False, stop=True)
                    c.inc("tensor", "pe", mmsv)
                    c.wait("vector", "pe", pe_b + job + 1)
                    c.inc("vector", "dve",
                          lambda q, tt=tt, ob=ob, pw=pw, k0=k0:
                          q.tensor_copy(v_tm[:, (k0 // 128 + tt) * ND + ob * 512:
                                             (k0 // 128 + tt) * ND + (ob + 1) * 512],
                                        ps_av[pw][:, 0:512]))
                    job += 1
            mseg = c.mark()
            c.wait("scalar", "act", mseg["act"])
            normsq_row(k_fm, k0, Tk, SEGK)
            mn = c.mark()
            c.wait("vector", "pe", mn["pe"])
            c.sinc("vector", "dve",
                   lambda q: q.reduce_max(nrow[0:1, 0:1], ps_n[0:1, 0:SEGK],
                                          axis=mybir.AxisListType.X))
            c.sinc("vector", "dve",
                   lambda q: q.tensor_tensor(Mk2[0:1, 0:1], Mk2[0:1, 0:1],
                                             nrow[0:1, 0:1], ALU.max))
            barrier_all(c, ("tensor", "scalar", "vector"))
        mmk = c.mark()
        c.wait("scalar", "dve", mmk["dve"])
        c.inc("scalar", "act",
              lambda q: q.activation(Mk[0:1, 0:1], Mk2[0:1, 0:1], AF.Sqrt))
        c.wait_marks("gpsimd", c.mark(), only=("pe",))
        c.dma("gpsimd", wpr_sb[:, :NC8 * ND].rearrange("p (c o) -> p c o", c=8),
              wpr.rearrange("(c p) o -> p c o", p=128))
        barrier_all(c, ENGINES)

        wat_q = wat_k
        qn_seg = xn_seg
        c.dma("gpsimd", wat_q[:, :].rearrange("p (c o) -> p c o", c=8),
              wat[:, 0:ND].rearrange("(c p) o -> p c o", p=128))

        for qs in range(Tq // SEGQ):
            q0 = qs * SEGQ
            if qs > 0:
                c.wait_marks("gpsimd", c.mark(), only=("pe",))
            for ch in range(NC8):
                c.dma("gpsimd", qn_seg[:, ch * SEGQ:(ch + 1) * SEGQ],
                      qnT[128 * ch:128 * (ch + 1), q0:q0 + SEGQ])
            m = c.mark()
            c.wait("tensor", "gpd", m["gpd"])
            nb = max(1, SEGQ // 512)
            wb = SEGQ // nb
            pe_b, act_b = c.cnt["pe"], c.cnt["act"]
            job = 0
            for mm_ in range(NC8):
                for tb in range(nb):
                    pw = job % 2
                    if job >= 2:
                        c.wait("tensor", "act", act_b + job - 1)

                    def mmsq(q, mm_=mm_, tb=tb, pw=pw, wb=wb):
                        for ch in range(NC8):
                            r = q.matmul(ps_s[pw][:, 0:wb],
                                         wat_q[:, ch * ND + 128 * mm_:ch * ND + 128 * (mm_ + 1)],
                                         qn_seg[:, ch * SEGQ + tb * wb:ch * SEGQ + (tb + 1) * wb],
                                         start=(ch == 0), stop=(ch == NC8 - 1))
                        return r
                    c.inc("tensor", "pe", mmsq)
                    c.wait("scalar", "pe", pe_b + job + 1)
                    c.inc("scalar", "act",
                          lambda q, mm_=mm_, tb=tb, pw=pw, wb=wb:
                          q.activation(q_fm[:, mm_ * SEGQ + tb * wb:
                                            mm_ * SEGQ + (tb + 1) * wb],
                                       ps_s[pw][:, 0:wb], AF.Identity,
                                       bias=bat_q[:, mm_:mm_ + 1]))
                    job += 1
            mq = c.mark()
            c.wait("scalar", "act", mq["act"])
            normsq_row(q_fm, 0, SEGQ, SEGQ)
            mq3 = c.mark()
            c.wait("scalar", "pe", mq3["pe"])
            c.inc("scalar", "act",
                  lambda q: q.activation(nrow[0:1, 0:SEGQ], ps_n[0:1, 0:SEGQ],
                                         AF.Sqrt))
            mq4 = c.mark()
            c.wait("vector", "act", mq4["act"])
            c.inc("vector", "dve",
                  lambda q: q.tensor_scalar(negb[0:1, 0:SEGQ], nrow[0:1, 0:SEGQ],
                                            Mk[0:1, 0:1], -1.0, ALU.mult, ALU.mult))
            barrier_all(c, ("tensor", "scalar", "vector"))

            for qb in range(SEGQ // WQ):
                qg = q0 + qb * WQ
                first_mask = qg // 128
                last_kt = min(NKT - 1, (qoff_max + qg + WQ - 1) // 128)
                kts = list(range(last_kt + 1))
                c.sinc("vector", "dve",
                       lambda q, qg=qg: q.tensor_scalar(IJq[:, :], IJ0[:, :],
                                                        qoff_sb[:, 0:1], float(qg),
                                                        ALU.add, ALU.add))
                pe_b, act_b, dve_b = c.cnt["pe"], c.cnt["act"], c.cnt["dve"]
                for j, kt in enumerate(kts):
                    pw = j % 2
                    if j >= 2:
                        c.wait("tensor", "act", act_b + j - 1)

                    def mms_s(q, kt=kt, pw=pw, qb=qb):
                        for ch in range(NC8):
                            q.matmul(ps_s[pw][:, 0:WQ],
                                     k_fm[:, ch * Tk + kt * 128:ch * Tk + (kt + 1) * 128],
                                     q_fm[:, ch * SEGQ + qb * WQ:ch * SEGQ + (qb + 1) * WQ],
                                     start=(ch == 0), stop=False)
                        return q.matmul(ps_s[pw][:, 0:WQ], ones_r[0:1, :],
                                        negb[0:1, qb * WQ:(qb + 1) * WQ],
                                        start=False, stop=True)
                    c.inc("tensor", "pe", mms_s)
                    c.wait("scalar", "pe", pe_b + j + 1)
                    c.inc("scalar", "act",
                          lambda q, kt=kt, pw=pw:
                          q.activation(P_T[:, kt * WQ:(kt + 1) * WQ],
                                       ps_s[pw][:, 0:WQ], AF.Exp, scale=1.0 / 32.0))
                    if kt >= first_mask:
                        c.wait("vector", "act", act_b + j + 1)
                        c.inc("vector", "dve",
                              lambda q, kt=kt:
                              q.scalar_tensor_tensor(P_T[:, kt * WQ:(kt + 1) * WQ],
                                                     IJq[:, :], float(kt * 128),
                                                     P_T[:, kt * WQ:(kt + 1) * WQ],
                                                     ALU.is_ge, ALU.mult))
                mqb = c.mark()
                c.wait("tensor", "act", mqb["act"])
                c.wait("tensor", "dve", mqb["dve"])

                def mm_den(q, kts=tuple(kts)):
                    for j, kt in enumerate(kts):
                        r = q.matmul(ps_d[0:1, 0:WQ], ones_c[:, 0:1],
                                     P_T[:, kt * WQ:(kt + 1) * WQ],
                                     start=(j == 0), stop=(j == len(kts) - 1))
                    return r
                c.inc("tensor", "pe", mm_den)
                mden = c.mark()
                c.wait("vector", "pe", mden["pe"])
                c.inc("vector", "dve",
                      lambda q: q.reciprocal(rd_sb[0:1, 0:WQ], ps_d[0:1, 0:WQ]))
                mrec = c.mark()
                c.wait("tensor", "dve", mrec["dve"])
                c.inc("tensor", "pe",
                      lambda q: q.matmul(ps_n[:, 0:WQ], ones_r32[0:1, :],
                                         rd_sb[0:1, 0:WQ], start=True, stop=True))
                mbc = c.mark()
                c.wait("vector", "pe", mbc["pe"])
                c.sinc("vector", "dve",
                       lambda q: q.tensor_copy(bc_sb[:, :], ps_n[:, 0:WQ]))
                pe_avb, dve_avb = c.cnt["pe"], c.cnt["dve"]
                for ch in range(NC8):
                    pw = ch % 2
                    if ch >= 2:
                        c.wait("tensor", "dve", dve_avb + ch - 1)

                    def mm_av(q, ch=ch, pw=pw, kts=tuple(kts)):
                        for j, kt in enumerate(kts):
                            r = q.matmul(ps_av[pw][:, 0:WQ],
                                         v_tm[:, kt * ND + ch * 128:kt * ND + (ch + 1) * 128],
                                         P_T[:, kt * WQ:(kt + 1) * WQ],
                                         start=(j == 0), stop=(j == len(kts) - 1))
                        return r
                    c.inc("tensor", "pe", mm_av)
                    c.wait("vector", "pe", pe_avb + ch + 1)
                    c.inc("vector", "dve",
                          lambda q, ch=ch, pw=pw:
                          q.tensor_tensor(av_qb[:, ch * WQ:(ch + 1) * WQ],
                                          ps_av[pw][:, 0:WQ], bc_sb[:, :], ALU.mult))
                mav = c.mark()
                c.wait("tensor", "dve", mav["dve"])
                for tt in range(WQ // 128):
                    pe_ob, dve_ob = c.cnt["pe"], c.cnt["dve"]
                    for ob in range(ND // 512):
                        pw = ob % 2

                        def mm_o(q, tt=tt, ob=ob, pw=pw):
                            for ch in range(NC8):
                                q.matmul(ps_o[pw][:, 0:512],
                                         av_qb[:, ch * WQ + tt * 128:ch * WQ + (tt + 1) * 128],
                                         wpr_sb[:, ch * ND + ob * 512:ch * ND + (ob + 1) * 512],
                                         start=(ch == 0), stop=False)
                            return q.matmul(ps_o[pw][:, 0:512], ones_r[0:1, :],
                                            bpr_row[0:1, ob * 512:(ob + 1) * 512],
                                            start=False, stop=True)
                        c.inc("tensor", "pe", mm_o)
                        c.wait("vector", "pe", pe_ob + ob + 1)
                        c.inc("vector", "dve",
                              lambda q, ob=ob, pw=pw:
                              q.tensor_copy(outb[:, ob * 512:(ob + 1) * 512],
                                            ps_o[pw][:, 0:512]))
                    mo = c.mark()
                    c.wait("sync", "dve", mo["dve"])
                    c.dma("sync", ya[qg + tt * 128:qg + (tt + 1) * 128, :], outb[:, :])
                    mo2 = c.mark()
                    c.wait("vector", "syd", mo2["syd"])
                barrier_all(c, ("tensor", "scalar", "vector"))
        mf = c.mark()
        c.wait("sync", "syd", mf["syd"])
        c.emit()
    return nc


PFATT_UPTO = "all"
KPAD = 512
KPT = KPAD // 128


def build_pffn_attn(R):
    nc = bass.Bass("TRN2", target_bir_lowering=False, debug=False,
                   detect_race_conditions=False)
    es = contextlib.ExitStack()
    tT = nc.dram_tensor("tT", [ND, R], F32, kind="ExternalInput").ap()
    wqkv = nc.dram_tensor("wqkv", [ND, 1536], F32, kind="ExternalInput").ap()
    wo = nc.dram_tensor("wo", [512, ND], F32, kind="ExternalInput").ap()
    qscl = nc.dram_tensor("qscl", [128, 2], F32, kind="ExternalInput").ap()
    qssw = nc.dram_tensor("qssw", [128, 2], F32, kind="ExternalInput").ap()
    tabcq = nc.dram_tensor("tabcq", [128, R], F32, kind="ExternalInput").ap()
    tabsq = nc.dram_tensor("tabsq", [128, R], F32, kind="ExternalInput").ap()
    tabck = nc.dram_tensor("tabck", [128, R], F32, kind="ExternalInput").ap()
    tabsk = nc.dram_tensor("tabsk", [128, R], F32, kind="ExternalInput").ap()
    kvalid = nc.dram_tensor("kvalid", [128, KPT + 5], F32, kind="ExternalInput").ap()
    onesat_in = nc.dram_tensor("onesat", [2, 128], F32, kind="ExternalInput").ap()
    t2T = nc.dram_tensor("t2T", [ND, R], F32, kind="ExternalOutput").ap()

    NT_ = R // 128
    NKT_ = NT_ + KPT
    NC8 = 8
    RP = R + 1
    NBLK = (R + 511) // 512

    def blkw(b):
        return min(512, R - 512 * b)

    with es:
        c = Ctx(nc, es)
        t_fm = c.sb("t_fm", [128, NC8 * RP], BF16)
        wq_sb = c.sb("wq_sb", [128, NC8 * 1536], BF16)
        q_fm = c.sb("q_fm", [128, 4 * R], BF16)
        k_fm = c.sb("k_fm", [128, 4 * (KPAD + R)], BF16)
        v_aug = c.sb("v_aug", [128, NKT_ * 520], BF16)
        tabs4 = c.sb("tabs4", [128, 4 * R], BF16)
        tq_c = tabs4[:, 0 * R:1 * R]
        tq_s = tabs4[:, 1 * R:2 * R]
        tk_c = tabs4[:, 2 * R:3 * R]
        tk_s = tabs4[:, 3 * R:4 * R]
        qs_c = c.sb("qs_c", [128, 2], F32)
        qs_w = c.sb("qs_w", [128, 2], F32)
        kval = c.sb("kval", [128, KPT + 5], F32)
        wo_sb = c.sb("wo_sb", [128, 4 * ND], BF16)
        Pm = c.sb("Pm", [128, 128], BF16)
        onesA = c.sb("onesA", [128, 2], BF16)
        onesAT = c.sb("onesAT", [2, 128], F32)
        rl = c.sb("rl", [2, 2048], F32)
        ones_r = c.sb("ones_r", [1, 128], BF16)
        IJ0 = c.sb("IJ0", [128, 128], F32)
        scr = c.sb("scr", [128, 1024], BF16)
        scr2 = c.sb("scr2", [128, 512], BF16)
        scr3 = c.sb("scr3", [128, 512], BF16)
        P_T = c.sb("P_T", [128, 5 * 1024], BF16)
        kcm = c.sb("kcm", [128, 4 * NKT_], F32)
        kbm = c.sb("kbm", [128, 4 * NT_], BF16)
        kbm2 = c.sb("kbm2", [128, 4 * NT_], BF16)
        nrow = c.sb("nrow", [1, 1024], F32)
        negb = c.sb("negb", [128, 1024], BF16)
        ones2 = c.sb("ones2", [128, 128], BF16)
        av_fm = tabs4
        t32 = c.sb("t32", [128, 512], F32)
        out32 = c.sb("out32", [128, 512], F32)
        tiny = c.sb("tiny", [1, 128], F32)
        epsb = c.sb("epsb", [128, 1], F32)

        ps_s = [c.ps("ps_s0", 2), c.ps("ps_s1", 2)]
        ps_a = [c.ps("ps_a0"), c.ps("ps_a1")]
        ps_r = c.ps("ps_r")
        ps_b = c.ps("ps_b")

        for ch in range(NC8):
            c.dma("gpsimd", t_fm[:, ch * RP + 1:ch * RP + 1 + R],
                  tT[128 * ch:128 * (ch + 1), :])
        c.dma("gpsimd", wq_sb[:, :].rearrange("p (c o) -> p c o", c=8),
              wqkv.rearrange("(c p) o -> p c o", p=128))
        c.dma("gpsimd", wo_sb[:, :].rearrange("p (c o) -> p c o", c=4),
              wo.rearrange("(c p) o -> p c o", p=128))
        c.dma("gpsimd", tq_c[:, :], tabcq)
        c.dma("gpsimd", tq_s[:, :], tabsq)
        c.dma("gpsimd", tk_c[:, :], tabck)
        c.dma("gpsimd", tk_s[:, :], tabsk)
        c.dma("sync", qs_c[:, :], qscl)
        c.dma("sync", qs_w[:, :], qssw)
        c.dma("sync", kval[:, :], kvalid)
        c.dma("sync", onesAT[:, :], onesat_in)

        def gp_init(q):
            q.memset(v_aug[:, :], 0.0)
            q.memset(bass.AP(v_aug, 64, [[520 * NKT_, 128], [520, NKT_], [65, 8]]), 1.0)
            q.memset(bass.AP(t_fm, 0, [[NC8 * RP, 128], [RP, NC8], [1, 1]]), 0.0)
            q.memset(ones_r[:, :], 1.0)
            q.memset(onesA[:, :], 0.0)
            q.memset(onesA[0:64, 0:1], 1.0)
            q.memset(onesA[64:128, 1:2], 1.0)
            q.memset(epsb[:, :], 1e-30)
            q.memset(ones2[:, :], 1.0)
            q.memset(bass.AP(k_fm, 0, [[4 * (KPAD + R), 128], [KPAD + R, 4], [1, KPAD]]), 0.0)
            return q.iota(IJ0[:, :], [[1, 128]], channel_multiplier=-1,
                          allow_small_or_imprecise_dtypes=True)
        c.inc("gpsimd", "gpe", gp_init)
        half = c.sb("halfm", [128, 128], F32)

        def gp_pm(q):
            q.iota(half[:, :], [[0, 2], [1, 2], [0, 32]], channel_multiplier=0,
                   allow_small_or_imprecise_dtypes=True)
            return q.nop()
        c.inc("gpsimd", "gpe", gp_pm)
        mk0 = c.mark()
        c.wait("vector", "gpe", mk0["gpe"])

        c.sinc("vector", "dve",
               lambda q: q.tensor_scalar(scr[:, 0:128], IJ0[:, :], -32.0,
                                         None, ALU.is_equal))
        c.sinc("vector", "dve",
               lambda q: q.tensor_scalar(scr[:, 128:256], half[:, :], 0.5,
                                         None, ALU.is_lt))
        c.sinc("vector", "dve",
               lambda q: q.tensor_tensor(scr[:, 0:128], scr[:, 0:128],
                                         scr[:, 128:256], ALU.mult))
        c.sinc("vector", "dve",
               lambda q: q.tensor_scalar(scr[:, 256:384], IJ0[:, :], 32.0,
                                         None, ALU.is_equal))
        c.sinc("vector", "dve",
               lambda q: q.tensor_scalar(scr[:, 384:512], half[:, :], 0.5,
                                         None, ALU.is_ge))
        c.sinc("vector", "dve",
               lambda q: q.tensor_tensor(scr[:, 256:384], scr[:, 256:384],
                                         scr[:, 384:512], ALU.mult))
        c.sinc("vector", "dve",
               lambda q: q.tensor_scalar(scr[:, 0:128], scr[:, 0:128], -1.0,
                                         None, ALU.mult))
        c.sinc("vector", "dve",
               lambda q: q.tensor_tensor(Pm[:, :], scr[:, 0:128],
                                         scr[:, 256:384], ALU.add))
        m0 = c.mark()
        c.wait("vector", "syd", m0["syd"])
        c.wait("vector", "gpd", m0["gpd"])
        c.sinc("vector", "dve",
               lambda q: q.tensor_scalar(tq_c[:, :], tq_c[:, :], qs_c[:, 0:1],
                                         None, ALU.mult))
        c.sinc("vector", "dve",
               lambda q: q.tensor_scalar(tq_s[:, :], tq_s[:, :], qs_w[:, 0:1],
                                         None, ALU.mult))
        c.sinc("vector", "dve",
               lambda q: q.tensor_scalar(tk_c[:, :], tk_c[:, :], qs_c[:, 1:2],
                                         None, ALU.mult))
        c.sinc("vector", "dve",
               lambda q: q.tensor_scalar(tk_s[:, :], tk_s[:, :], qs_w[:, 1:2],
                                         None, ALU.mult))
        barrier_all(c, ENGINES)

        def shifted(ch, t0, w):
            off = 0 if ch >= 4 else 1
            return t_fm[:, ch * RP + off + t0:ch * RP + off + t0 + w]

        for b in range(NBLK):
            w = blkw(b)
            t0 = b * 512
            pe_b, act_b = c.cnt["pe"], c.cnt["act"]
            job = 0
            for dst, col0 in ((q_fm, 0), (k_fm, KPAD)):
                for mm_ in range(4):
                    pw = job % 2
                    if job >= 2:
                        c.wait("tensor", "act", act_b + job - 1)
                    wcol = (0 if dst is q_fm else 512) + mm_ * 128

                    def mms(q, mm_=mm_, pw=pw, wcol=wcol, t0=t0, w=w):
                        for ch in range(NC8):
                            r = q.matmul(ps_a[pw][:, 0:w],
                                         wq_sb[:, ch * 1536 + wcol:ch * 1536 + wcol + 128],
                                         shifted(ch, t0, w),
                                         start=(ch == 0), stop=(ch == NC8 - 1))
                        return r
                    c.inc("tensor", "pe", mms)
                    c.wait("scalar", "pe", pe_b + job + 1)
                    stride = R if dst is q_fm else KPAD + R
                    c.inc("scalar", "act",
                          lambda q, dst=dst, mm_=mm_, pw=pw, col0=col0,
                          stride=stride, t0=t0, w=w:
                          q.activation(dst[:, mm_ * stride + col0 + t0:
                                           mm_ * stride + col0 + t0 + w],
                                       ps_a[pw][:, 0:w], AF.Identity))
                    job += 1
            c.wait("tensor", "act", c.cnt["act"])
            pe_b2, dve_b2 = c.cnt["pe"], c.cnt["dve"]
            vjob = 0
            for tt in range(w // 128):
                pw = vjob % 2
                if vjob >= 2:
                    c.wait("tensor", "dve", dve_b2 + vjob - 1)

                def mmsv(q, tt=tt, pw=pw, t0=t0):
                    for ch in range(NC8):
                        r = q.matmul(ps_a[pw][:, 0:512],
                                     shifted(ch, t0 + tt * 128, 128),
                                     wq_sb[:, ch * 1536 + 1024:ch * 1536 + 1536],
                                     start=(ch == 0), stop=(ch == NC8 - 1))
                    return r
                c.inc("tensor", "pe", mmsv)
                c.wait("vector", "pe", pe_b2 + vjob + 1)
                gkt = KPT + (t0 // 128) + tt
                c.inc("vector", "dve",
                      lambda q, pw=pw, gkt=gkt:
                      q.tensor_copy(bass.AP(v_aug, gkt * 520, [[520 * NKT_, 128], [65, 8], [1, 64]]),
                                    ps_a[pw][:, 0:512]))
                vjob += 1
            barrier_all(c, ("tensor", "scalar", "vector"))

        def norm_rotary(dst, col0, stride, tabc, tabs):
            for b in range(NBLK):
                w = blkw(b)
                t0 = b * 512
                for ch in range(4):
                    mW = c.mark()
                    c.wait("scalar", "pe", mW["pe"])
                    c.wait("scalar", "dve", mW["dve"])
                    c.inc("scalar", "act",
                          lambda q, ch=ch, t0=t0, w=w:
                          q.activation(scr[:, 0:w],
                                       dst[:, ch * stride + col0 + t0:
                                           ch * stride + col0 + t0 + w],
                                       AF.Square))
                    c.wait("tensor", "act", c.cnt["act"])
                    c.wait("tensor", "dve", c.cnt["dve"])
                    c.inc("tensor", "pe",
                          lambda q, w=w:
                          q.matmul(ps_r[0:2, 0:w], onesA[:, 0:2], scr[:, 0:w],
                                   start=True, stop=True, skip_group_check=True))
                    c.wait("vector", "pe", c.cnt["pe"])
                    c.sinc("vector", "dve",
                           lambda q, ch=ch, w=w:
                           q.tensor_scalar(rl[0:2, ch * 512:ch * 512 + w],
                                           ps_r[0:2, 0:w], 1e-24, None, ALU.max))
                mL = c.mark()
                c.wait("scalar", "dve", mL["dve"])
                c.inc("scalar", "act",
                      lambda q: q.activation(rl[0:2, 0:2048], rl[0:2, 0:2048],
                                             AF.Ln))
                c.inc("scalar", "act",
                      lambda q: q.activation(rl[0:2, 0:2048], rl[0:2, 0:2048],
                                             AF.Exp, scale=-0.5))
                mn3 = c.mark()
                dve_cb = c.cnt["dve"]
                for ch in range(4):
                    src = dst[:, ch * stride + col0 + t0:ch * stride + col0 + t0 + w]
                    c.wait("tensor", "act", mn3["act"])
                    c.wait("tensor", "dve", c.cnt["dve"])
                    c.inc("tensor", "pe",
                          lambda q, src=src, w=w:
                          q.matmul(ps_a[0][:, 0:w], Pm[:, :], src,
                                   start=True, stop=True))
                    c.inc("tensor", "pe",
                          lambda q, ch=ch, w=w:
                          q.matmul(ps_b[:, 0:w], onesAT[:, :],
                                   rl[0:2, ch * 512:ch * 512 + w],
                                   start=True, stop=True))
                    mr = c.mark()
                    c.wait("vector", "pe", mr["pe"])

                    c.sinc("vector", "dve",
                           lambda q, src=src, t0=t0, w=w, tabc=tabc:
                           q.tensor_tensor(scr2[:, 0:w], src,
                                           tabc[:, t0:t0 + w], ALU.mult))
                    c.sinc("vector", "dve",
                           lambda q, t0=t0, w=w, tabs=tabs:
                           q.tensor_tensor(scr3[:, 0:w], ps_a[0][:, 0:w],
                                           tabs[:, t0:t0 + w], ALU.mult))
                    c.sinc("vector", "dve",
                           lambda q, w=w:
                           q.tensor_tensor(scr2[:, 0:w], scr2[:, 0:w],
                                           scr3[:, 0:w], ALU.add))
                    c.sinc("vector", "dve",
                           lambda q, src=src, w=w:
                           q.tensor_tensor(src, scr2[:, 0:w], ps_b[:, 0:w],
                                           ALU.mult))

        if PFATT_UPTO == "s1":
            c.emit()
            return nc
        norm_rotary(q_fm, 0, R, tq_c, tq_s)
        norm_rotary(k_fm, KPAD, KPAD + R, tk_c, tk_s)
        barrier_all(c, ENGINES)
        if PFATT_UPTO == "s2":
            c.emit()
            return nc

        KR = KPAD + R
        for kt in range(NKT_):
            for ch in range(4):
                c.inc("vector", "dve",
                      lambda q, ch=ch, kt=kt:
                      q.reduce_max(kcm[:, ch * NKT_ + kt:ch * NKT_ + kt + 1],
                                   k_fm[:, ch * KR + kt * 128:ch * KR + (kt + 1) * 128],
                                   axis=mybir.AxisListType.X,
                                   apply_absolute_value=True))
        mS = c.mark()
        c.wait("vector", "dve", mS["dve"])
        c.sinc("vector", "dve",
               lambda q: q.tensor_tensor(
                   bass.AP(kbm, 0, [[4 * NT_, 128], [NT_, 4], [1, NT_]]),
                   bass.AP(kcm, 0, [[4 * NKT_, 128], [NKT_, 4], [1, NT_]]),
                   bass.AP(kcm, 1, [[4 * NKT_, 128], [NKT_, 4], [1, NT_]]), ALU.max))
        c.sinc("vector", "dve",
               lambda q: q.tensor_tensor(
                   bass.AP(kbm2, 0, [[4 * NT_, 128], [NT_, 4], [1, NT_]]),
                   bass.AP(kbm, 0, [[4 * NT_, 128], [NT_, 4], [1, NT_]]),
                   bass.AP(kcm, 2, [[4 * NKT_, 128], [NKT_, 4], [1, NT_]]), ALU.max))
        c.sinc("vector", "dve",
               lambda q: q.tensor_tensor(
                   bass.AP(kbm, 0, [[4 * NT_, 128], [NT_, 4], [1, NT_]]),
                   bass.AP(kbm2, 0, [[4 * NT_, 128], [NT_, 4], [1, NT_]]),
                   bass.AP(kcm, 3, [[4 * NKT_, 128], [NKT_, 4], [1, NT_]]), ALU.max))
        c.sinc("vector", "dve",
               lambda q: q.tensor_tensor(
                   bass.AP(kbm2, 0, [[4 * NT_, 128], [NT_, 4], [1, NT_]]),
                   bass.AP(kbm, 0, [[4 * NT_, 128], [NT_, 4], [1, NT_]]),
                   bass.AP(kcm, 4, [[4 * NKT_, 128], [NKT_, 4], [1, NT_]]), ALU.max))
        c.sinc("vector", "dve",
               lambda q: q.tensor_scalar(kbm[:, :], kbm2[:, :], -1.0, None,
                                         ALU.mult))
        barrier_all(c, ENGINES)
        if PFATT_UPTO == "s3a":
            c.emit()
            return nc

        for qt in range(NT_):
            for ch in range(4):
                mW = c.mark()
                c.wait("scalar", "pe", mW["pe"])
                c.inc("scalar", "act",
                      lambda q, ch=ch, qt=qt:
                      q.activation(scr[:, ch * 128:(ch + 1) * 128],
                                   q_fm[:, ch * R + qt * 128:ch * R + (qt + 1) * 128],
                                   AF.Abs))
            mW = c.mark()
            c.wait("tensor", "act", mW["act"])
            c.wait("tensor", "dve", c.cnt["dve"])
            for p in range(2):
                pssl = ps_s[p]
                for ch in range(4):
                    h = 2 * ch + p
                    c.inc("tensor", "pe",
                          lambda q, ch=ch, p=p, pssl=pssl, qt=qt:
                          q.matmul(pssl[0:1, ch * 128:(ch + 1) * 128],
                                   kbm[64 * p:64 * p + 64,
                                       ch * NT_ + qt:ch * NT_ + qt + 1],
                                   scr[64 * p:64 * p + 64, ch * 128:(ch + 1) * 128],
                                   start=True, stop=True, skip_group_check=True,
                                   tile_position=(64 * p, 0)))
                c.wait("vector", "pe", c.cnt["pe"])
                if p == 0:
                    c.sinc("vector", "dve",
                           lambda q, pssl=pssl:
                           q.tensor_copy(negb[0:1, 0:512], pssl[0:1, 0:512]))
                else:
                    c.sinc("vector", "dve",
                           lambda q, pssl=pssl:
                           q.tensor_copy(negb[64:65, 512:1024], pssl[0:1, 0:512]))
            if PFATT_UPTO == "s3b_negb":
                continue
            mnb = c.mark()
            c.wait("tensor", "dve", mnb["dve"])
            pe_b, act_b, dve_b = c.cnt["pe"], c.cnt["act"], c.cnt["dve"]
            for j in range(5):
                pw = j % 2
                if j >= 2:
                    c.wait("tensor", "act", act_b + 2 * (j - 1))
                ktp = qt + j

                def mms_s(q, j=j, pw=pw, ktp=ktp, qt=qt):
                    r = None
                    for p in range(2):
                        pssl = ps_s[p]
                        for ch in range(4):
                            q.matmul(pssl[:, pw * 512 + ch * 128:
                                          pw * 512 + (ch + 1) * 128],
                                     k_fm[64 * p:64 * p + 64,
                                          ch * KR + ktp * 128:ch * KR + (ktp + 1) * 128],
                                     q_fm[64 * p:64 * p + 64,
                                          ch * R + qt * 128:ch * R + (qt + 1) * 128],
                                     start=(ch == 0), stop=False,
                                     tile_position=(64 * p, 0),
                                     skip_group_check=True)
                        r = q.matmul(pssl[:, pw * 512:pw * 512 + 512],
                                     ones2[64 * p:64 * p + 1, :],
                                     negb[64 * p:64 * p + 1, p * 512:p * 512 + 512],
                                     start=False, stop=True,
                                     tile_position=(64 * p, 0),
                                     skip_group_check=True)
                    return r
                c.inc("tensor", "pe", mms_s)
                c.wait("scalar", "pe", c.cnt["pe"])
                for p in range(2):
                    c.inc("scalar", "act",
                          lambda q, j=j, pw=pw, p=p:
                          q.activation(P_T[:, j * 1024 + p * 512:
                                           j * 1024 + p * 512 + 512],
                                       ps_s[p][:, pw * 512:pw * 512 + 512],
                                       AF.Exp, scale=8.0))
                mexp = c.mark()
                c.wait("vector", "act", mexp["act"])
                mops = []
                if j == 0:
                    mops.append(ALU.is_le)
                if j == 4:
                    mops.append(ALU.is_ge)
                for op in mops:
                    c.inc("vector", "dve",
                          lambda q, j=j, op=op:
                          q.scalar_tensor_tensor(P_T[:, j * 1024:(j + 1) * 1024],
                                                 bass.AP(IJ0, 0, [[128, 128], [0, 8], [1, 128]]),
                                                 0.0, P_T[:, j * 1024:(j + 1) * 1024],
                                                 op, ALU.mult))
                if ktp < KPT + 5:
                    c.inc("vector", "dve",
                          lambda q, j=j, ktp=ktp:
                          q.tensor_scalar(P_T[:, j * 1024:(j + 1) * 1024],
                                          P_T[:, j * 1024:(j + 1) * 1024],
                                          kval[:, ktp:ktp + 1], None, ALU.mult))
            if PFATT_UPTO == "s3b_scores":
                continue
            mqb = c.mark()
            c.wait("tensor", "act", mqb["act"])
            c.wait("tensor", "dve", mqb["dve"])
            for h in range(8):
                pw = h % 2
                if h >= 2:
                    c.wait("tensor", "dve", c.mark()["dve"])

                def mm_av(q, h=h, pw=pw, qt=qt):
                    pos = (h % 2) * 512 + (h // 2) * 128
                    for j in range(5):
                        r = q.matmul(ps_a[pw][0:65, 0:128],
                                     v_aug[:, (qt + j) * 520 + h * 65:(qt + j) * 520 + (h + 1) * 65],
                                     P_T[:, j * 1024 + pos:j * 1024 + pos + 128],
                                     start=(j == 0), stop=(j == 4))
                    return r
                c.inc("tensor", "pe", mm_av)
                if PFATT_UPTO == "s3b_av":
                    continue
                mav = c.mark()
                c.wait("vector", "pe", mav["pe"])
                c.sinc("vector", "dve",
                       lambda q, pw=pw: q.tensor_scalar(tiny[0:1, 0:128],
                                                        ps_a[pw][64:65, 0:128],
                                                        1e-30, None, ALU.max))
                c.sinc("vector", "dve",
                       lambda q: q.reciprocal(tiny[0:1, 0:128], tiny[0:1, 0:128]))
                if PFATT_UPTO == "s3b_n1":
                    continue
                mrc = c.mark()
                c.wait("tensor", "dve", mrc["dve"])
                c.inc("tensor", "pe",
                      lambda q: q.matmul(ps_b[0:64, 0:128], onesAT[0:1, 0:64],
                                         tiny[0:1, 0:128], start=True, stop=True))
                if PFATT_UPTO == "s3b_n2":
                    continue
                mbc = c.mark()
                c.wait("vector", "pe", mbc["pe"])
                c.sinc("vector", "dve",
                       lambda q: q.tensor_copy(scr2[0:64, 0:128], ps_b[0:64, 0:128]))
                if PFATT_UPTO == "s3b_n3":
                    continue
                ch, p = h // 2, h % 2
                c.sinc("vector", "dve",
                       lambda q, ch=ch, p=p, pw=pw, qt=qt:
                       q.tensor_tensor(av_fm[64 * p:64 * p + 64,
                                             ch * R + qt * 128:ch * R + (qt + 1) * 128],
                                       ps_a[pw][0:64, 0:128], scr2[0:64, 0:128],
                                       ALU.mult))
            barrier_all(c, ("tensor", "scalar", "vector"))

        if PFATT_UPTO == "s3b":
            c.emit()
            return nc
        for b in range(NBLK):
            w = blkw(b)
            t0 = b * 512
            for oc in range(NC8):
                mW = c.mark()
                c.wait("sync", "dve", mW["dve"])
                c.dma("sync", t32[:, 0:w], tT[128 * oc:128 * (oc + 1), t0:t0 + w])
                pe_b = c.cnt["pe"]

                def mm_o(q, oc=oc, t0=t0, w=w):
                    for cin in range(4):
                        r = q.matmul(ps_a[0][:, 0:w],
                                     wo_sb[:, cin * ND + oc * 128:cin * ND + (oc + 1) * 128],
                                     av_fm[:, cin * R + t0:cin * R + t0 + w],
                                     start=(cin == 0), stop=(cin == 3))
                    return r
                c.wait("tensor", "dve", c.cnt["dve"])
                c.inc("tensor", "pe", mm_o)
                mo = c.mark()
                c.wait("vector", "pe", mo["pe"])
                c.wait("vector", "syd", mo["syd"])
                c.sinc("vector", "dve",
                       lambda q, w=w: q.tensor_tensor(out32[:, 0:w], ps_a[0][:, 0:w],
                                                      t32[:, 0:w], ALU.add))
                mo2 = c.mark()
                c.wait("sync", "dve", mo2["dve"])
                c.dma("sync", t2T[128 * oc:128 * (oc + 1), t0:t0 + w], out32[:, 0:w])
        mf = c.mark()
        c.wait("sync", "syd", mf["syd"])
        barrier_all(c, ENGINES)
        c.emit()
    return nc



HID = 2730
NFS = 22


def build_pffn_ffn(R):
    nc = bass.Bass("TRN2", target_bir_lowering=False, debug=False,
                   detect_race_conditions=False)
    es = contextlib.ExitStack()
    t2T = nc.dram_tensor("t2T", [ND, R], F32, kind="ExternalInput").ap()
    lngc = nc.dram_tensor("lngc", [128, 8], F32, kind="ExternalInput").ap()
    lnbc = nc.dram_tensor("lnbc", [128, 8], F32, kind="ExternalInput").ap()
    w1 = nc.dram_tensor("w1", [ND, 2 * HID], F32, kind="ExternalInput").ap()
    b1a = nc.dram_tensor("b1a", [128, NFS], F32, kind="ExternalInput").ap()
    b1g = nc.dram_tensor("b1g", [128, NFS], F32, kind="ExternalInput").ap()
    w2 = nc.dram_tensor("w2", [HID, ND], F32, kind="ExternalInput").ap()
    b2 = nc.dram_tensor("b2", [ND], F32, kind="ExternalInput").ap()
    t3T = nc.dram_tensor("t3T", [ND, R], F32, kind="ExternalOutput").ap()

    NC8 = 8
    RP = R + 1
    NBLK = (R + 511) // 512

    def blkw(b):
        return min(512, R - 512 * b)

    def fw(fs):
        return 128 if fs < NFS - 1 else HID - 128 * (NFS - 1)

    with es:
        c = Ctx(nc, es)
        t2_fm = c.sb("t2_fm", [128, NC8 * RP], BF16)
        w2_sb = c.sb("w2_sb", [128, NFS * ND], BF16)
        w1a = [c.sb(f"w1a{i}", [128, NC8 * 128], BF16) for i in range(2)]
        w1g = [c.sb(f"w1g{i}", [128, NC8 * 128], BF16) for i in range(2)]
        lng_sb = c.sb("lng_sb", [128, 8], F32)
        lnb_sb = c.sb("lnb_sb", [128, 8], F32)
        b1a_sb = c.sb("b1a_sb", [128, NFS], F32)
        b1g_sb = c.sb("b1g_sb", [128, NFS], F32)
        b2row = c.sb("b2row", [1, ND], BF16)
        ones_c = c.sb("ones_cb", [128, 1], BF16)
        ones_r = c.sb("ones_rb", [1, 128], BF16)
        ones_r32 = c.sb("ones_r32b", [1, 128], F32)
        ones_rw = c.sb("ones_rwb", [1, 512], BF16)
        scr = c.sb("scrb", [128, 512], BF16)
        scr2 = c.sb("scr2b", [128, 512], F32)
        h2_fm = c.sb("h2_fm", [128, NC8 * 512], BF16)
        hs_sb = c.sb("hs_sb", [128, NFS * 512], BF16)
        rows = c.sb("rowsb", [1, 1024], F32)
        mub = c.sb("mub", [128, 512], F32)
        rsb = c.sb("rsb", [128, 512], F32)
        t32 = c.sb("t32b", [128, 512], F32)
        out32 = c.sb("out32b", [128, 512], F32)
        aa = c.sb("aab", [128, 512], BF16)
        gg = c.sb("ggb", [128, 512], BF16)

        ps_m = [c.ps("ps_m0"), c.ps("ps_m1")]
        ps_o = [c.ps("ps_o0"), c.ps("ps_o1")]
        ps_r = c.ps("ps_rb", 2)
        ps_b = c.ps("ps_bb")

        for ch in range(NC8):
            c.dma("gpsimd", t2_fm[:, ch * RP + 1:ch * RP + 1 + R],
                  t2T[128 * ch:128 * (ch + 1), :])
        c.dma("gpsimd",
              w2_sb[:, 0:(NFS - 1) * ND].rearrange("p (f o) -> p f o", f=NFS - 1),
              w2[0:(NFS - 1) * 128, :].rearrange("(f p) o -> p f o", p=128))
        c.dma("gpsimd", w2_sb[0:HID - (NFS - 1) * 128, (NFS - 1) * ND:NFS * ND],
              w2[(NFS - 1) * 128:HID, :])
        c.dma("sync", lng_sb[:, :], lngc)
        c.dma("sync", lnb_sb[:, :], lnbc)
        c.dma("sync", b1a_sb[:, :], b1a)
        c.dma("sync", b1g_sb[:, :], b1g)
        c.dma("gpsimd", b2row[0:1, :], b2[None, :])

        def gp_init(q):
            q.memset(bass.AP(t2_fm, 0, [[NC8 * RP, 128], [RP, NC8], [1, 1]]), 0.0)
            q.memset(ones_c[:, :], 1.0)
            q.memset(ones_r[:, :], 1.0)
            q.memset(ones_rw[:, :], 1.0)
            return q.memset(ones_r32[:, :], 1.0)
        c.inc("gpsimd", "gpe", gp_init)
        barrier_all(c, ENGINES)

        def shifted(ch, t0, w):
            off = 0 if ch >= 4 else 1
            return t2_fm[:, ch * RP + off + t0:ch * RP + off + t0 + w]

        c.dma("gpsimd", w1a[0][:, :].rearrange("p (c o) -> p c o", c=8),
              w1[:, 0:128].rearrange("(c p) o -> p c o", p=128))
        c.dma("gpsimd", w1g[0][:, :].rearrange("p (c o) -> p c o", c=8),
              w1[:, HID:HID + 128].rearrange("(c p) o -> p c o", p=128))
        w1_ready = c.cnt["gpd"]

        for b in range(NBLK):
            w = blkw(b)
            t0 = b * 512
            pe_b = c.cnt["pe"]
            for ch in range(NC8):
                c.inc("tensor", "pe",
                      lambda q, ch=ch, t0=t0, w=w:
                      q.matmul(ps_r[0:1, 0:w], ones_c[:, 0:1], shifted(ch, t0, w),
                               start=(ch == 0), stop=(ch == NC8 - 1),
                               skip_group_check=True))
            for ch in range(NC8):
                if ch >= 1:
                    c.wait("scalar", "pe", pe_b + NC8 + ch)
                c.inc("scalar", "act",
                      lambda q, ch=ch, t0=t0, w=w:
                      q.activation(scr[:, 0:w], shifted(ch, t0, w), AF.Square))
                c.wait("tensor", "act", c.cnt["act"])
                c.inc("tensor", "pe",
                      lambda q, ch=ch, w=w:
                      q.matmul(ps_r[0:1, 512:512 + w], ones_c[:, 0:1], scr[:, 0:w],
                               start=(ch == 0), stop=(ch == NC8 - 1),
                               skip_group_check=True))
            mS = c.mark()
            c.wait("vector", "pe", mS["pe"])
            c.sinc("vector", "dve",
                   lambda q, w=w: q.tensor_scalar(rows[0:1, 0:w], ps_r[0:1, 0:w],
                                                  1.0 / ND, None, ALU.mult))
            c.sinc("vector", "dve",
                   lambda q, w=w: q.tensor_tensor(scr2[0:1, 0:w], rows[0:1, 0:w],
                                                  rows[0:1, 0:w], ALU.mult))
            c.sinc("vector", "dve",
                   lambda q, w=w: q.tensor_scalar(rows[0:1, 512:512 + w],
                                                  ps_r[0:1, 512:512 + w],
                                                  1.0 / ND, None, ALU.mult))
            c.sinc("vector", "dve",
                   lambda q, w=w: q.tensor_tensor(rows[0:1, 512:512 + w],
                                                  rows[0:1, 512:512 + w],
                                                  scr2[0:1, 0:w], ALU.subtract))
            c.sinc("vector", "dve",
                   lambda q, w=w: q.tensor_scalar(rows[0:1, 512:512 + w],
                                                  rows[0:1, 512:512 + w],
                                                  1e-5, None, ALU.add))
            mS2 = c.mark()
            c.wait("scalar", "dve", mS2["dve"])
            c.inc("scalar", "act",
                  lambda q, w=w: q.activation(rows[0:1, 512:512 + w],
                                              rows[0:1, 512:512 + w], AF.Ln))
            c.inc("scalar", "act",
                  lambda q, w=w: q.activation(rows[0:1, 512:512 + w],
                                              rows[0:1, 512:512 + w],
                                              AF.Exp, scale=-0.5))
            mS3 = c.mark()
            c.wait("tensor", "act", mS3["act"])
            c.inc("tensor", "pe",
                  lambda q, w=w: q.matmul(ps_b[:, 0:w], ones_r32[0:1, :],
                                          rows[0:1, 0:w], start=True, stop=True))
            mb1 = c.mark()
            c.wait("vector", "pe", mb1["pe"])
            c.sinc("vector", "dve",
                   lambda q, w=w: q.tensor_copy(mub[:, 0:w], ps_b[:, 0:w]))
            c.wait("tensor", "dve", c.cnt["dve"])
            c.inc("tensor", "pe",
                  lambda q, w=w: q.matmul(ps_b[:, 0:w], ones_r32[0:1, :],
                                          rows[0:1, 512:512 + w],
                                          start=True, stop=True))
            mb2 = c.mark()
            c.wait("vector", "pe", mb2["pe"])
            c.sinc("vector", "dve",
                   lambda q, w=w: q.tensor_copy(rsb[:, 0:w], ps_b[:, 0:w]))
            for ch in range(NC8):
                c.sinc("vector", "dve",
                       lambda q, ch=ch, t0=t0, w=w:
                       q.tensor_tensor(scr2[:, 0:w], shifted(ch, t0, w),
                                       mub[:, 0:w], ALU.subtract))
                c.sinc("vector", "dve",
                       lambda q, w=w:
                       q.tensor_tensor(scr2[:, 0:w], scr2[:, 0:w], rsb[:, 0:w],
                                       ALU.mult))
                c.sinc("vector", "dve",
                       lambda q, ch=ch, w=w:
                       q.tensor_scalar(h2_fm[:, ch * 512:ch * 512 + w],
                                       scr2[:, 0:w], lng_sb[:, ch:ch + 1],
                                       lnb_sb[:, ch:ch + 1], ALU.mult, ALU.add))
            mln = c.mark()
            c.wait("tensor", "dve", mln["dve"])
            for fs in range(NFS):
                cw = fw(fs)
                par = fs % 2
                this_ready = w1_ready
                if fs + 1 < NFS or b + 1 < NBLK:
                    nfs = (fs + 1) % NFS
                    ncw = fw(nfs)
                    c.wait_marks("gpsimd", {"pe": c.cnt["pe"]})
                    c.dma("gpsimd",
                          w1a[(par + 1) % 2][:, 0:NC8 * ncw].rearrange(
                              "p (c o) -> p c o", c=8),
                          w1[:, nfs * 128:nfs * 128 + ncw].rearrange(
                              "(c p) o -> p c o", p=128))
                    c.dma("gpsimd",
                          w1g[(par + 1) % 2][:, 0:NC8 * ncw].rearrange(
                              "p (c o) -> p c o", c=8),
                          w1[:, HID + nfs * 128:HID + nfs * 128 + ncw].rearrange(
                              "(c p) o -> p c o", p=128))
                    w1_ready = c.cnt["gpd"]
                c.wait("tensor", "gpd", this_ready)
                c.wait("tensor", "act", c.cnt["act"])

                def mm1(q, fs=fs, cw=cw, par=par, w=w):
                    for ch in range(NC8):
                        q.matmul(ps_m[0][0:cw, 0:w],
                                 w1a[par][:, ch * cw:(ch + 1) * cw],
                                 h2_fm[:, ch * 512:ch * 512 + w],
                                 start=(ch == 0), stop=(ch == NC8 - 1),
                                 skip_group_check=True)
                    for ch in range(NC8):
                        r = q.matmul(ps_m[1][0:cw, 0:w],
                                     w1g[par][:, ch * cw:(ch + 1) * cw],
                                     h2_fm[:, ch * 512:ch * 512 + w],
                                     start=(ch == 0), stop=(ch == NC8 - 1),
                                     skip_group_check=True)
                    return r
                c.inc("tensor", "pe", mm1)
                mm1m = c.mark()
                c.wait("scalar", "pe", mm1m["pe"])
                c.wait("scalar", "dve", c.cnt["dve"])
                c.inc("scalar", "act",
                      lambda q, fs=fs, cw=cw, w=w:
                      q.activation(aa[0:cw, 0:w], ps_m[0][0:cw, 0:w], AF.Identity,
                                   bias=b1a_sb[:, fs:fs + 1][0:cw, 0:1]))
                c.inc("scalar", "act",
                      lambda q, fs=fs, cw=cw, w=w:
                      q.activation(gg[0:cw, 0:w], ps_m[1][0:cw, 0:w], AF.Gelu,
                                   bias=b1g_sb[:, fs:fs + 1][0:cw, 0:1]))
                mg = c.mark()
                c.wait("vector", "act", mg["act"])
                c.inc("vector", "dve",
                      lambda q, fs=fs, cw=cw, w=w:
                      q.tensor_tensor(hs_sb[0:cw, fs * 512:fs * 512 + w],
                                      aa[0:cw, 0:w], gg[0:cw, 0:w], ALU.mult))
            mh = c.mark()
            c.wait("tensor", "dve", mh["dve"])
            for oc in range(NC8):
                po = oc % 2
                c.wait("sync", "dve", c.cnt["dve"])
                c.dma("sync", t32[:, 0:w], t2T[128 * oc:128 * (oc + 1), t0:t0 + w])
                c.wait("tensor", "dve", c.cnt["dve"])

                def mm2(q, oc=oc, po=po, w=w):
                    for fs in range(NFS):
                        cw = fw(fs)
                        q.matmul(ps_o[po][:, 0:w],
                                 w2_sb[0:cw, fs * ND + oc * 128:fs * ND + oc * 128 + 128],
                                 hs_sb[0:cw, fs * 512:fs * 512 + w],
                                 start=(fs == 0), stop=False,
                                 skip_group_check=True)
                    return q.matmul(ps_o[po][:, 0:w],
                                    b2row[0:1, oc * 128:(oc + 1) * 128],
                                    ones_rw[0:1, 0:w],
                                    start=False, stop=True, skip_group_check=True)
                c.inc("tensor", "pe", mm2)
                mo = c.mark()
                c.wait("vector", "pe", mo["pe"])
                c.wait("vector", "syd", mo["syd"])
                c.sinc("vector", "dve",
                       lambda q, po=po, w=w:
                       q.tensor_tensor(out32[:, 0:w], ps_o[po][:, 0:w],
                                       t32[:, 0:w], ALU.add))
                mo2 = c.mark()
                c.wait("sync", "dve", mo2["dve"])
                c.dma("sync", t3T[128 * oc:128 * (oc + 1), t0:t0 + w], out32[:, 0:w])
            barrier_all(c, ("tensor", "scalar", "vector"))
        mf = c.mark()
        c.wait("sync", "syd", mf["syd"])
        c.emit()
    return nc


def build_assemble_ln(R):
    nc = bass.Bass("TRN2", target_bir_lowering=False, debug=False,
                   detect_race_conditions=False)
    es = contextlib.ExitStack()
    x = nc.dram_tensor("x", [R, ND], F32, kind="ExternalInput").ap()
    ya = nc.dram_tensor("ya", [R, ND], F32, kind="ExternalInput").ap()
    yf = nc.dram_tensor("yf", [R, ND], F32, kind="ExternalInput").ap()
    x1 = nc.dram_tensor("x1", [R, ND], F32, kind="ExternalOutput").ap()
    xn2 = nc.dram_tensor("xn2", [R, ND], F32, kind="ExternalOutput").ap()
    ntile = R // 128
    with es:
        c = Ctx(nc, es)
        xb = [c.sb(f"xb{i}", [128, ND], F32) for i in range(2)]
        yab = [c.sb(f"yab{i}", [128, ND], F32) for i in range(2)]
        yfb = [c.sb(f"yfb{i}", [128, ND], F32) for i in range(2)]
        sb1 = [c.sb(f"sb1{i}", [128, ND], F32) for i in range(2)]
        ob = [c.sb(f"ob{i}", [128, ND], F32) for i in range(2)]
        scr = c.sb("scr", [128, ND], F32)
        s1 = c.sb("s1", [128, 1], F32)
        s2 = c.sb("s2", [128, 1], F32)
        mu = c.sb("mu", [128, 1], F32)
        var = c.sb("var", [128, 1], F32)
        rstd = c.sb("rstd", [128, 1], F32)
        DVE_PER = 9
        for i in range(ntile):
            p = i % 2
            if i >= 2:
                c.wait("sync", "dve", DVE_PER * (i - 1))
            c.dma("sync", xb[p][:, :], x[128 * i:128 * (i + 1), :])
            c.dma("sync", yab[p][:, :], ya[128 * i:128 * (i + 1), :])
            c.dma("sync", yfb[p][:, :], yf[128 * i:128 * (i + 1), :])
            m = c.mark()
            c.wait("vector", "syd", m["syd"])
            if i >= 2:
                c.wait("vector", "syd", 80 * (i - 1))
            c.sinc("vector", "dve",
                   lambda q, p=p: q.tensor_tensor(sb1[p][:, :], xb[p][:, :],
                                                  yab[p][:, :], ALU.add))
            c.sinc("vector", "dve",
                   lambda q, p=p: q.tensor_tensor(sb1[p][:, :], sb1[p][:, :],
                                                  yfb[p][:, :], ALU.add))
            c.inc("vector", "dve",
                  lambda q, p=p: q.reduce_sum(s1[:, 0:1], sb1[p][:, :],
                                              axis=mybir.AxisListType.X))
            m2a = c.mark()
            c.wait("scalar", "dve", m2a["dve"])
            c.inc("scalar", "act",
                  lambda q, p=p: q.activation(scr[:, :], sb1[p][:, :], AF.Square,
                                              accum_out=s2[:, 0:1]))
            m2 = c.mark()
            c.wait("vector", "act", m2["act"])
            c.sinc("vector", "dve",
                   lambda q: q.tensor_scalar(mu[:, 0:1], s1[:, 0:1], 1.0 / ND,
                                             None, ALU.mult))
            c.sinc("vector", "dve",
                   lambda q: q.tensor_scalar(var[:, 0:1], s2[:, 0:1], 1.0 / ND,
                                             None, ALU.mult))
            c.sinc("vector", "dve",
                   lambda q: q.tensor_tensor(scr[:, 0:1], mu[:, 0:1], mu[:, 0:1],
                                             ALU.mult))
            c.sinc("vector", "dve",
                   lambda q: q.tensor_scalar(var[:, 0:1], var[:, 0:1], 1e-5,
                                             scr[:, 0:1], ALU.add, ALU.subtract))
            c.sinc("vector", "dve",
                   lambda q: q.reciprocal(rstd[:, 0:1], var[:, 0:1]))
            m2b = c.mark()
            c.wait("scalar", "dve", m2b["dve"])
            c.inc("scalar", "act",
                  lambda q: q.activation(rstd[:, 0:1], rstd[:, 0:1], AF.Sqrt))
            m3 = c.mark()
            c.wait("vector", "act", m3["act"])
            c.inc("vector", "dve",
                  lambda q, p=p: q.tensor_scalar(ob[p][:, :], sb1[p][:, :],
                                                 mu[:, 0:1], rstd[:, 0:1],
                                                 ALU.subtract, ALU.mult))
            m4 = c.mark()
            c.wait("sync", "dve", m4["dve"])
            c.dma("sync", x1[128 * i:128 * (i + 1), :], sb1[p][:, :])
            c.dma("sync", xn2[128 * i:128 * (i + 1), :], ob[p][:, :])
        mf = c.mark()
        c.wait("sync", "syd", mf["syd"])
        c.emit()
    return nc


QMAP = {
    (0, 0): (0, 1), (0, 1): (1, -1), (0, 2): (2, -1), (0, 3): (3, -1),
    (1, 0): (1, 1), (1, 1): (0, 1), (1, 2): (3, -1), (1, 3): (2, 1),
    (2, 0): (2, 1), (2, 1): (3, 1), (2, 2): (0, 1), (2, 3): (1, -1),
    (3, 0): (3, 1), (3, 1): (2, -1), (3, 2): (1, 1), (3, 3): (0, 1),
}
QMAP = {
    0: [(0, 0, 1), (1, 1, -1), (2, 2, -1), (3, 3, -1)],
    1: [(1, 0, 1), (0, 1, 1), (3, 2, 1), (2, 3, -1)],
    2: [(2, 0, 1), (3, 1, -1), (0, 2, 1), (1, 3, 1)],
    3: [(3, 0, 1), (2, 1, 1), (1, 2, -1), (0, 3, 1)],
}
QBIAS = {
    0: [1, -1, -1, -1],
    1: [1, 1, 1, -1],
    2: [1, -1, 1, 1],
    3: [1, 1, -1, 1],
}


L1_DEBUG = False
L1_PARANOID = False
L1_NONORM = False
L1_SKIPS23 = False
L1_NOVB = False
L1_NOJ = False
L1_NOV = False
L1_NOQ = False


def build_l1_attn(Tq, Tk):
    nc = bass.Bass("TRN2", target_bir_lowering=False, debug=False,
                   detect_race_conditions=False)
    es = contextlib.ExitStack()
    xnT = nc.dram_tensor("xnT", [ND, Tk], F32, kind="ExternalInput").ap()
    qnT = nc.dram_tensor("qnT", [ND, Tq], F32, kind="ExternalInput").ap()
    Ws = {n: nc.dram_tensor(n, [4, 256, 256], F32, kind="ExternalInput").ap()
          for n in ("qW", "kW", "vW", "oW")}
    Bs = {n: nc.dram_tensor(n, [4, 256], F32, kind="ExternalInput").ap()
          for n in ("qb", "kb", "vb", "ob_")}
    Jin = nc.dram_tensor("J", [ND], F32, kind="ExternalInput").ap()
    ya2T = nc.dram_tensor("ya2T", [ND, Tq], F32, kind="ExternalOutput").ap()
    if L1_DEBUG:
        dbg_av = nc.dram_tensor("dbg_av", [128, 8 * 512], F32, kind="ExternalOutput").ap()
        dbg_q = nc.dram_tensor("dbg_q", [128, 8 * Tq], F32, kind="ExternalOutput").ap()
        dbg_k = nc.dram_tensor("dbg_k", [128, 8 * Tk], F32, kind="ExternalOutput").ap()
        dbg_v = nc.dram_tensor("dbg_v", [128, (Tk // 128) * 16 * 65], F32, kind="ExternalOutput").ap()
        dbg_w = nc.dram_tensor("dbg_w", [128, 2048], F32, kind="ExternalOutput").ap()
        dbg_wn = nc.dram_tensor("dbg_wn", [128, 2048], F32, kind="ExternalOutput").ap()
        dbg_b = nc.dram_tensor("dbg_b", [128, 8], F32, kind="ExternalOutput").ap()
        dbg_b2 = nc.dram_tensor("dbg_b2", [128, 8], F32, kind="ExternalOutput").ap()
    vbounce = nc.dram_tensor("vbounce", [1, ND], F32).ap()

    NC8 = 8
    NKT = Tk // 128
    NH = 16
    SEG = min(Tk, 512)
    WQ1 = min(512, Tq)

    with es:
        c = Ctx(nc, es)
        c.paranoid = L1_PARANOID
        k_fm = c.sb("k_fm", [128, NC8 * Tk], BF16)
        q_fm = c.sb("q_fm", [128, NC8 * WQ1], BF16)
        v_aug = c.sb("v_aug", [128, NKT * (NH * 65)], BF16)
        xn_seg = c.sb("xn_seg", [128, NC8 * SEG], BF16)
        av_qb = c.sb("av_qb", [128, NC8 * 512], BF16)
        wsb = {n: c.sb(f"wsb_{n}", [128, 2048], BF16) for n in ("qW", "kW", "vW")}
        wng = {n: c.sb(f"wng_{n}", [128, 2048], BF16) for n in ("qW", "kW", "vW")}
        wsb["oW"] = wsb["kW"]
        wng["oW"] = wng["kW"]
        bcol = {n: c.sb(f"bcol_{n}", [128, 8], F32) for n in ("qb", "kb", "vb", "ob_")}
        bcmb = {n: c.sb(f"bcmb_{n}", [128, 8], F32) for n in ("qb", "kb", "vb", "ob_")}
        vbrow = c.sb("vbrow", [1, ND], BF16)
        ones_r = c.sb("ones_r1", [1, 128], BF16)
        ones_rw = c.sb("ones_rw1", [1, 512], BF16)
        ones_c32 = c.sb("ones_c32", [128, 1], F32)
        ones_r64 = c.sb("ones_r64", [1, 64], F32)
        ones_rf = c.sb("ones_rf", [1, 128], F32)
        jrow = c.sb("jrow", [128, 8], F32)
        c16 = c.sb("c16", [128, 1], F32)
        eps6 = c.sb("eps6", [128, 1], F32)
        scr = c.sb("scr1", [128, 512], BF16)
        scrf = c.sb("scrf", [128, 512], F32)
        scrf2 = c.sb("scrf2", [128, 512], F32)
        scr2f = c.sb("scr2f", [128, 512], F32)
        P_kt = c.sb("P_kt", [128, 4 * 512], BF16)
        tiny = c.sb("tiny1", [1, 512], F32)
        bc64 = c.sb("bc64", [128, 512], F32)
        t32o = c.sb("t32o", [128, 512], F32)
        av_acc = c.sb("av_acc", [65, 512], F32)

        ps_s = [c.ps("ps_s0"), c.ps("ps_s1")]
        ps_t = [c.ps("ps_t0"), c.ps("ps_t1")]
        ps_av = c.ps("ps_av")
        ps_a = [c.ps("ps_a0"), c.ps("ps_a1")]
        ps_b = c.ps("ps_b")
        ps_r = ps_b

        for n in ("qW", "kW", "vW"):
            for w_ in range(4):
                for s_ in range(2):
                    c.dma("gpsimd", wsb[n][:, (w_ * 2 + s_) * 256:(w_ * 2 + s_ + 1) * 256],
                          Ws[n][w_, s_ * 128:(s_ + 1) * 128, :])
        for n in ("qb", "kb", "vb", "ob_"):
            for w_ in range(4):
                for s_ in range(2):
                    c.dma("gpsimd", bcol[n][:, w_ * 2 + s_:w_ * 2 + s_ + 1],
                          Bs[n][w_, s_ * 128:(s_ + 1) * 128][:, None],
                          allow_slow_non_contiguous=True)
        c.dma("gpsimd", jrow[:, :], Jin.rearrange("(c p) -> p c", p=128),
              allow_slow_non_contiguous=True)

        def gp_init(q):
            q.memset(ones_r[:, :], 1.0)
            q.memset(ones_rw[:, :], 1.0)
            q.memset(ones_c32[:, :], 1.0)
            q.memset(eps6[:, :], 1e-6)
            q.memset(av_acc[:, :], 0.0)
            q.memset(ones_r64[:, :], 1.0)
            q.memset(ones_rf[:, :], 1.0)
            return q.memset(v_aug[:, :], 1.0)
        c.inc("gpsimd", "gpe", gp_init)
        m0 = c.mark()
        c.wait("vector", "gpd", m0["gpd"])
        for n in ("qW", "kW", "vW"):
            c.inc("vector", "dve",
                  lambda q, n=n: q.tensor_scalar(wng[n][:, :], wsb[n][:, :],
                                                 -1.0, None, ALU.mult))
        for n in ("qb", "kb", "vb", "ob_"):
            for m in range(4):
                for sub in range(2):
                    sgns = QBIAS[m]
                    d = bcmb[n][:, m * 2 + sub:m * 2 + sub + 1]
                    c.sinc("vector", "dve",
                           lambda q, d=d, n=n, sub=sub, s0=float(sgns[0]):
                           q.tensor_scalar(d, bcol[n][:, sub:sub + 1],
                                           s0, None, ALU.mult))
                    for w_ in range(1, 4):
                        op = ALU.add if sgns[w_] > 0 else ALU.subtract
                        c.sinc("vector", "dve",
                               lambda q, d=d, n=n, w_=w_, sub=sub, op=op:
                               q.tensor_tensor(d, d, bcol[n][:, w_ * 2 + sub:
                                                             w_ * 2 + sub + 1], op))
        if L1_NOVB:
            c.inc("gpsimd", "gpe", lambda q: q.memset(vbrow[:, :], 0.0))
        else:
            mvb = c.mark()
            c.wait("sync", "dve", mvb["dve"])
            c.dma("sync", vbounce.rearrange("o (c p) -> o p c", p=128),
                  bcmb["vb"][:, :], allow_slow_non_contiguous=True)
            mvb2 = c.mark()
            c.wait("gpsimd", "syd", mvb2["syd"])
            c.dma("gpsimd", vbrow[0:1, :], vbounce)
        if L1_NOJ:
            c.inc("gpsimd", "gpe", lambda q: q.memset(c16[:, :], -16.0))
        else:
            c.wait("scalar", "gpd", m0["gpd"])
            c.inc("scalar", "act",
                  lambda q: q.activation(jrow[:, :], jrow[:, :], AF.Sin))
            mj = c.mark()
            c.wait("vector", "act", mj["act"])
            c.sinc("vector", "dve",
                   lambda q: q.reduce_sum(c16[:, 0:1], jrow[:, :],
                                          axis=mybir.AxisListType.X))
            mj2 = c.mark()
            c.wait("tensor", "dve", mj2["dve"])
            c.inc("tensor", "pe",
                  lambda q: q.matmul(ps_r[0:1, 0:1], ones_c32[:, 0:1], c16[:, 0:1],
                                     start=True, stop=True))
            mj3 = c.mark()
            c.wait("vector", "pe", mj3["pe"])
            c.sinc("vector", "dve",
                   lambda q: q.tensor_scalar(tiny[0:1, 0:1], ps_r[0:1, 0:1],
                                             0.1 / ND, None, ALU.mult))
            mj4 = c.mark()
            c.wait("tensor", "dve", mj4["dve"])
            c.inc("tensor", "pe",
                  lambda q: q.matmul(ps_b[:, 0:1], ones_rf[0:1, :],
                                     tiny[0:1, 0:1], start=True, stop=True))
            mj5 = c.mark()
            c.wait("vector", "pe", mj5["pe"])
            c.sinc("vector", "dve",
                   lambda q: q.tensor_scalar(c16[:, 0:1], ps_b[:, 0:1], -16.0,
                                             None, ALU.add))
        barrier_all(c, ENGINES)

        def quat_proj(dst, dst_stride, col0, src, src_w, wkey, bkey, norm):
            nb = (src_w + 511) // 512
            for tb in range(nb):
                w_ = min(512, src_w - tb * 512)
                for m in range(4):
                    for sub in range(2):
                        pw = (m * 2 + sub) % 2
                        c.wait("tensor", "act", c.cnt["act"])
                        c.wait("tensor", "dve", c.cnt["dve"])

                        def mmq(q, m=m, sub=sub, tb=tb, w_=w_, pw=pw, wkey=wkey):
                            first = True
                            for (cblk, w_idx, sign) in QMAP[m]:
                                wt = wsb[wkey] if sign > 0 else wng[wkey]
                                for s2_ in range(2):
                                    r = q.matmul(
                                        ps_a[pw][:, 0:w_],
                                        wt[:, (w_idx * 2 + s2_) * 256 + sub * 128:
                                           (w_idx * 2 + s2_) * 256 + sub * 128 + 128],
                                        src[:, (cblk * 2 + s2_) * src_w + tb * 512:
                                            (cblk * 2 + s2_) * src_w + tb * 512 + w_],
                                        start=first, stop=(cblk, s2_) == (QMAP[m][-1][0], 1),
                                        skip_group_check=True)
                                    first = False
                            return r
                        c.inc("tensor", "pe", mmq)
                        c.wait("scalar", "pe", c.cnt["pe"])
                        c.inc("scalar", "act",
                              lambda q, m=m, sub=sub, tb=tb, w_=w_, pw=pw,
                              bkey=bkey:
                              q.activation(dst[:, (m * 2 + sub) * dst_stride +
                                               col0 + tb * 512:
                                               (m * 2 + sub) * dst_stride +
                                               col0 + tb * 512 + w_],
                                           ps_a[pw][:, 0:w_], AF.Identity,
                                           bias=bcmb[bkey][:, m * 2 + sub:
                                                           m * 2 + sub + 1]))
                        barrier_all(c, ("tensor", "scalar", "vector"))
                if norm:
                    mq = c.mark()
                    c.wait("vector", "act", mq["act"])
                    for sub in range(2):
                        for m in range(4):
                            sl = dst[:, (m * 2 + sub) * dst_stride + col0 + tb * 512:
                                     (m * 2 + sub) * dst_stride + col0 + tb * 512 + w_]
                            if m == 0:
                                c.sinc("vector", "dve",
                                       lambda q, sl=sl, w_=w_:
                                       q.tensor_tensor(scrf[:, 0:w_], sl, sl,
                                                       ALU.mult))
                            else:
                                c.sinc("vector", "dve",
                                       lambda q, sl=sl, w_=w_:
                                       q.tensor_tensor(scr2f[:, 0:w_], sl, sl,
                                                       ALU.mult))
                                c.sinc("vector", "dve",
                                       lambda q, w_=w_:
                                       q.tensor_tensor(scrf[:, 0:w_], scrf[:, 0:w_],
                                                       scr2f[:, 0:w_], ALU.add))
                        mqs = c.mark()
                        c.wait("scalar", "dve", mqs["dve"])
                        c.inc("scalar", "act",
                              lambda q, w_=w_:
                              q.activation(scrf2[:, 0:w_], scrf[:, 0:w_], AF.Ln,
                                           bias=eps6[:, 0:1]))
                        c.inc("scalar", "act",
                              lambda q, w_=w_:
                              q.activation(scrf2[:, 0:w_], scrf2[:, 0:w_],
                                           AF.Exp, scale=-0.5))
                        mqe = c.mark()
                        c.wait("vector", "act", mqe["act"])
                        for m in range(4):
                            sl = dst[:, (m * 2 + sub) * dst_stride + col0 + tb * 512:
                                     (m * 2 + sub) * dst_stride + col0 + tb * 512 + w_]
                            c.sinc("vector", "dve",
                                   lambda q, sl=sl, w_=w_:
                                   q.tensor_tensor(sl, sl, scrf2[:, 0:w_],
                                                   ALU.mult))

        for seg in range(Tk // SEG):
            k0 = seg * SEG
            if seg > 0:
                c.wait_marks("gpsimd", c.mark(), only=("pe",))
            for ch in range(NC8):
                c.dma("gpsimd", xn_seg[:, ch * SEG:(ch + 1) * SEG],
                      xnT[128 * ch:128 * (ch + 1), k0:k0 + SEG])
            m = c.mark()
            c.wait("tensor", "gpd", m["gpd"])
            quat_proj(k_fm, Tk, k0, xn_seg, SEG, "kW", "kb", not L1_NONORM)
            for tt in range(SEG // 128):
                mW = c.mark()
                c.wait("tensor", "dve", mW["dve"])
                c.wait("tensor", "act", mW["act"])

                def mmv(q, tt=tt):
                    r = None
                    for bank in range(2):
                        for mh in range(2):
                            m_ = bank * 2 + mh
                            for idx, (cblk, w_idx, sign) in enumerate(QMAP[m_]):
                                wt = wsb["vW"] if sign > 0 else wng["vW"]
                                for s2_ in range(2):
                                    q.matmul(
                                        ps_a[bank][:, mh * 256:mh * 256 + 256],
                                        xn_seg[:, (cblk * 2 + s2_) * SEG + tt * 128:
                                               (cblk * 2 + s2_) * SEG + tt * 128 + 128],
                                        wt[:, (w_idx * 2 + s2_) * 256:
                                           (w_idx * 2 + s2_) * 256 + 256],
                                        start=(mh == 0 and idx == 0 and s2_ == 0),
                                        stop=False, skip_group_check=True)
                        r = q.matmul(ps_a[bank][:, 0:512], ones_r[0:1, :],
                                     vbrow[0:1, bank * 512:bank * 512 + 512],
                                     start=False, stop=True,
                                     skip_group_check=True)
                    return r
                c.inc("tensor", "pe", mmv)
                mv = c.mark()
                c.wait("vector", "pe", mv["pe"])
                gkt = k0 // 128 + tt
                for bank in range(2):
                    c.sinc("vector", "dve",
                           lambda q, bank=bank, gkt=gkt:
                           q.tensor_copy(
                               bass.AP(v_aug, gkt * (NH * 65) + bank * 8 * 65,
                                       [[NKT * NH * 65, 128], [65, 8], [1, 64]]),
                               ps_a[bank][:, 0:512]))
                barrier_all(c, ("tensor", "scalar", "vector"))
        barrier_all(c, ENGINES)

        for w_ in range(4):
            for s_ in range(2):
                c.dma("gpsimd", wsb["kW"][:, (w_ * 2 + s_) * 256:
                                          (w_ * 2 + s_ + 1) * 256],
                      Ws["oW"][w_, s_ * 128:(s_ + 1) * 128, :])
        mo0 = c.mark()
        c.wait("vector", "gpd", mo0["gpd"])
        c.inc("vector", "dve",
              lambda q: q.tensor_scalar(wng["kW"][:, :], wsb["kW"][:, :],
                                        -1.0, None, ALU.mult))
        barrier_all(c, ENGINES)

        def q_proj_block(qb):
            c.wait_marks("gpsimd", c.mark(), only=("pe",))
            for ch in range(NC8):
                c.dma("gpsimd", xn_seg[:, ch * WQ1:(ch + 1) * WQ1],
                      qnT[128 * ch:128 * (ch + 1), qb * WQ1:(qb + 1) * WQ1])
            m = c.mark()
            c.wait("tensor", "gpd", m["gpd"])
            quat_proj(q_fm, WQ1, 0, xn_seg, WQ1, "qW", "qb", not L1_NONORM)
            barrier_all(c, ("tensor", "scalar", "vector"))

        if L1_DEBUG:
            if not L1_NOQ:
                q_proj_block(0)
            c.dma("gpsimd", dbg_w, wsb["qW"][:, :])
            c.dma("gpsimd", dbg_wn, wng["qW"][:, :])
            c.dma("gpsimd", dbg_b, bcmb["qb"][:, :])
            c.dma("gpsimd", dbg_b2, bcol["qb"][:, :])
            c.dma("gpsimd", dbg_q, q_fm[:, :])
            c.dma("gpsimd", dbg_k, k_fm[:, :])
            c.dma("gpsimd", dbg_v, v_aug[:, :])
            c.wait("gpsimd", "gpd", c.cnt["gpd"])

        if L1_SKIPS23:
            mf = c.mark()
            c.wait("sync", "syd", mf["syd"])
            c.emit()
            return nc
        KCH = min(4, NKT)
        for qb in range(Tq // WQ1):
            if not (L1_DEBUG and qb == 0):
                q_proj_block(qb)
            for h in range(NH):
                ch, p = h // 2, h % 2
                pst = ps_s if p == 0 else ps_t
                for c4 in range(NKT // KCH):
                    c.wait("tensor", "act", c.cnt["act"])
                    act_b = c.cnt["act"]
                    for j in range(KCH):
                        kt = c4 * KCH + j
                        pw = j % 2
                        if j >= 2:
                            c.wait("tensor", "act", act_b + j - 1)
                        c.inc("tensor", "pe",
                              lambda q, kt=kt, pw=pw, ch=ch, p=p, qb=qb, pst=pst:
                              q.matmul(pst[pw][:, 0:WQ1],
                                       k_fm[64 * p:64 * p + 64,
                                            ch * Tk + kt * 128:ch * Tk + (kt + 1) * 128],
                                       q_fm[64 * p:64 * p + 64,
                                            ch * WQ1:ch * WQ1 + WQ1],
                                       start=True, stop=True,
                                       tile_position=(64 * p, 0),
                                       skip_group_check=True))
                        c.wait("scalar", "pe", c.cnt["pe"])
                        c.inc("scalar", "act",
                              lambda q, j=j, pw=pw, pst=pst:
                              q.activation(P_kt[:, j * 512:j * 512 + WQ1],
                                           pst[pw][:, 0:WQ1], AF.Exp, scale=0.125,
                                           bias=c16[:, 0:1]))
                    mex = c.mark()
                    c.wait("tensor", "act", mex["act"])
                    c.wait("tensor", "dve", c.cnt["dve"])

                    def mm_avc(q, c4=c4, h=h, qb=qb):
                        for j in range(KCH):
                            kt = c4 * KCH + j
                            r = q.matmul(ps_av[0:65, 0:WQ1],
                                         v_aug[:, kt * (NH * 65) + h * 65:
                                               kt * (NH * 65) + (h + 1) * 65],
                                         P_kt[:, j * 512:j * 512 + WQ1],
                                         start=(j == 0), stop=(j == KCH - 1),
                                         skip_group_check=True)
                        return r
                    c.inc("tensor", "pe", mm_avc)
                    mava = c.mark()
                    c.wait("vector", "pe", mava["pe"])
                    if c4 == 0:
                        c.sinc("vector", "dve",
                               lambda q: q.tensor_copy(av_acc[0:65, 0:WQ1],
                                                       ps_av[0:65, 0:WQ1]))
                    else:
                        c.sinc("vector", "dve",
                               lambda q: q.tensor_tensor(av_acc[0:65, 0:WQ1],
                                                         av_acc[0:65, 0:WQ1],
                                                         ps_av[0:65, 0:WQ1],
                                                         ALU.add))
                    barrier_all(c, ("tensor", "scalar", "vector"))
                c.sinc("vector", "dve",
                       lambda q: q.tensor_scalar(tiny[0:1, 0:WQ1],
                                                 av_acc[64:65, 0:WQ1], 1e-30,
                                                 None, ALU.max))
                c.sinc("vector", "dve",
                       lambda q: q.reciprocal(tiny[0:1, 0:WQ1], tiny[0:1, 0:WQ1]))
                mrc = c.mark()
                c.wait("tensor", "dve", mrc["dve"])
                c.inc("tensor", "pe",
                      lambda q: q.matmul(ps_b[0:64, 0:WQ1], ones_r64[0:1, :],
                                         tiny[0:1, 0:WQ1], start=True, stop=True))
                mbc = c.mark()
                c.wait("vector", "pe", mbc["pe"])
                c.sinc("vector", "dve",
                       lambda q: q.tensor_copy(bc64[0:64, 0:WQ1], ps_b[0:64, 0:WQ1]))
                c.sinc("vector", "dve",
                       lambda q, ch=ch, p=p:
                       q.tensor_tensor(av_qb[64 * p:64 * p + 64,
                                             ch * WQ1:(ch + 1) * WQ1],
                                       av_acc[0:64, 0:WQ1], bc64[0:64, 0:WQ1],
                                       ALU.mult))
                barrier_all(c, ("tensor", "scalar", "vector"))
            mavq = c.mark()
            c.wait("tensor", "dve", mavq["dve"])
            for m in range(4):
                for sub in range(2):
                    pw = (m * 2 + sub) % 2

                    def mmo(q, m=m, sub=sub, qb=qb, pw=pw):
                        first = True
                        for (cblk, w_idx, sign) in QMAP[m]:
                            wt = wsb["oW"] if sign > 0 else wng["oW"]
                            for s2_ in range(2):
                                r = q.matmul(
                                    ps_a[pw][:, 0:WQ1],
                                    wt[:, (w_idx * 2 + s2_) * 256 + sub * 128:
                                       (w_idx * 2 + s2_) * 256 + sub * 128 + 128],
                                    av_qb[:, (cblk * 2 + s2_) * WQ1:
                                          (cblk * 2 + s2_ + 1) * WQ1],
                                    start=first, stop=(cblk, s2_) ==
                                    (QMAP[m][-1][0], 1),
                                    skip_group_check=True)
                                first = False
                        return r
                    c.wait("tensor", "dve", c.cnt["dve"])
                    c.inc("tensor", "pe", mmo)
                    mo = c.mark()
                    c.wait("vector", "pe", mo["pe"])
                    c.wait("vector", "syd", mo["syd"])
                    c.sinc("vector", "dve",
                           lambda q, m=m, sub=sub, pw=pw:
                           q.tensor_scalar(t32o[:, 0:WQ1], ps_a[pw][:, 0:WQ1],
                                           bcmb["ob_"][:, m * 2 + sub:
                                                       m * 2 + sub + 1],
                                           None, ALU.add))
                    mo2 = c.mark()
                    c.wait("sync", "dve", mo2["dve"])
                    c.dma("sync", ya2T[(m * 2 + sub) * 128:(m * 2 + sub + 1) * 128,
                                       qb * WQ1:qb * WQ1 + WQ1], t32o[:, 0:WQ1])
        mf = c.mark()
        c.wait("sync", "syd", mf["syd"])
        c.emit()
    return nc
def build_l1_ffn(Tq):
    nc = bass.Bass("TRN2", target_bir_lowering=False, debug=False,
                   detect_race_conditions=False)
    es = contextlib.ExitStack()
    xn2T = nc.dram_tensor("xn2T", [ND, Tq], F32, kind="ExternalInput").ap()
    x1 = nc.dram_tensor("x1", [Tq, ND], F32, kind="ExternalInput").ap()
    ya2 = nc.dram_tensor("ya2", [Tq, ND], F32, kind="ExternalInput").ap()
    xn2r = nc.dram_tensor("xn2r", [Tq, ND], F32, kind="ExternalInput").ap()
    last = nc.dram_tensor("last", [1, ND], F32, kind="ExternalInput").ap()
    Wp1 = nc.dram_tensor("Wp1", [ND, 128], F32, kind="ExternalInput").ap()
    bp1 = nc.dram_tensor("bp1", [128], F32, kind="ExternalInput").ap()
    Wp2 = nc.dram_tensor("Wp2", [128, 8], F32, kind="ExternalInput").ap()
    bp2 = nc.dram_tensor("bp2", [8], F32, kind="ExternalInput").ap()
    W1t = nc.dram_tensor("W1", [8, ND, 2048], F32, kind="ExternalInput")
    b1t = nc.dram_tensor("b1", [8, 2048], F32, kind="ExternalInput")
    W2t = nc.dram_tensor("W2", [8, 2048, ND], F32, kind="ExternalInput")
    b2t = nc.dram_tensor("b2", [8, ND], F32, kind="ExternalInput")
    Wo = nc.dram_tensor("Wo", [ND, ND], F32, kind="ExternalInput").ap()
    bo = nc.dram_tensor("bo", [ND], F32, kind="ExternalInput").ap()
    x2 = nc.dram_tensor("x2", [Tq, ND], F32, kind="ExternalOutput").ap()

    NC8 = 8
    NF = 16
    NTB = Tq // 512

    with es:
        c = Ctx(nc, es)
        xn2_fm = c.sb("xn2_fm", [128, NC8 * Tq], BF16)
        w1sel = c.sb("w1sel", [128, NC8 * 2048], BF16)
        h_fm = c.sb("h_fm", [128, NF * Tq], BF16)
        w2sel = c.sb("w2sel", [128, NF * ND], BF16)
        r_fm = xn2_fm
        wo_sb = c.sb("wo_sb", [128, NC8 * ND], BF16)
        lcol = c.sb("lcol", [128, 8], BF16)
        wp1_sb = c.sb("wp1_sb", [128, NC8 * 128], BF16)
        wp2_sb = c.sb("wp2_sb", [128, 8], BF16)
        bp1c = c.sb("bp1c", [128, 1], F32)
        bp2r = c.sb("bp2r", [1, 8], F32)
        b1c = c.sb("b1c", [128, NF], F32)
        b2c = c.sb("b2c", [128, NC8], F32)
        borow = c.sb("borow", [1, ND], BF16)
        hcol = c.sb("hcol", [128, 1], BF16)
        lg = c.sb("lg", [1, 8], F32)
        riota = c.sb("riota", [1, 8], F32)
        selmax = c.sb("selmax", [1, 1], F32)
        self_f = c.sb("self_f", [1, 1], F32)
        self_i = c.sb("self_i", [1, 1], mybir.dt.int32)
        ones_r = c.sb("ones_rh", [1, 128], BF16)
        x1t = c.sb("x1t", [128, ND], F32)
        ya2t = c.sb("ya2t", [128, ND], F32)
        xn2t = c.sb("xn2t", [128, ND], F32)
        out32 = c.sb("out32h", [128, ND], F32)

        ps_a = [c.ps("ps_ah0"), c.ps("ps_ah1")]
        ps_r = c.ps("ps_rh")

        c.dma("gpsimd", lcol[:, :], last.rearrange("o (c p) -> p (o c)", p=128),
              allow_slow_non_contiguous=True)
        c.dma("gpsimd", wp1_sb[:, :].rearrange("p (c o) -> p c o", c=8),
              Wp1.rearrange("(c p) o -> p c o", p=128))
        c.dma("gpsimd", wp2_sb[:, :], Wp2)
        c.dma("sync", bp1c[:, 0:1], bp1.rearrange("(p o) -> p o", o=1))
        c.dma("sync", bp2r[0:1, :], bp2[None, :])
        c.dma("gpsimd", borow[0:1, :], bo[None, :])
        c.dma("gpsimd", wo_sb[:, :].rearrange("p (c o) -> p c o", c=8),
              Wo.rearrange("(c p) o -> p c o", p=128))
        c.inc("gpsimd", "gpe", lambda q: q.memset(ones_r[:, :], 1.0))
        c.inc("gpsimd", "gpe",
              lambda q: q.iota(riota[:, :], [[1, 8]], channel_multiplier=0,
                               allow_small_or_imprecise_dtypes=True))
        m0 = c.mark()
        c.wait("tensor", "gpd", m0["gpd"])
        c.wait("tensor", "syd", m0["syd"])

        def mmr(q):
            for ch in range(NC8):
                r = q.matmul(ps_r[0:128, 0:1],
                             wp1_sb[:, ch * 128:(ch + 1) * 128],
                             lcol[:, ch:ch + 1],
                             start=(ch == 0), stop=(ch == NC8 - 1))
            return r
        c.inc("tensor", "pe", mmr)
        m1 = c.mark()
        c.wait("scalar", "pe", m1["pe"])
        c.inc("scalar", "act",
              lambda q: q.activation(hcol[:, 0:1], ps_r[0:128, 0:1], AF.Gelu,
                                     bias=bp1c[:, 0:1]))
        m2 = c.mark()
        c.wait("tensor", "act", m2["act"])
        c.inc("tensor", "pe",
              lambda q: q.matmul(ps_r[0:1, 0:8], hcol[:, 0:1], wp2_sb[:, :],
                                 start=True, stop=True))
        m3 = c.mark()
        c.wait("vector", "pe", m3["pe"])
        c.wait("vector", "syd", m3["syd"])
        c.wait("vector", "gpe", m3["gpe"])
        c.sinc("vector", "dve",
               lambda q: q.tensor_tensor(lg[0:1, :], ps_r[0:1, 0:8], bp2r[0:1, :],
                                         ALU.add))
        c.sinc("vector", "dve",
               lambda q: q.reduce_max(selmax[0:1, 0:1], lg[0:1, :],
                                      axis=mybir.AxisListType.X))
        c.sinc("vector", "dve",
               lambda q: q.tensor_scalar(lg[0:1, :], lg[0:1, :],
                                         selmax[0:1, 0:1], None, ALU.is_ge))
        c.sinc("vector", "dve",
               lambda q: q.tensor_scalar(riota[0:1, :], riota[0:1, :], -1.0, 8.0,
                                         ALU.mult, ALU.add))
        c.sinc("vector", "dve",
               lambda q: q.tensor_tensor(lg[0:1, :], lg[0:1, :], riota[0:1, :],
                                         ALU.mult))
        c.sinc("vector", "dve",
               lambda q: q.reduce_max(selmax[0:1, 0:1], lg[0:1, :],
                                      axis=mybir.AxisListType.X))
        c.sinc("vector", "dve",
               lambda q: q.tensor_scalar(self_f[0:1, 0:1], selmax[0:1, 0:1],
                                         -1.0, 8.0, ALU.mult, ALU.add))
        c.sinc("vector", "dve",
               lambda q: q.tensor_copy(self_i[0:1, 0:1], self_f[0:1, 0:1]))
        mi = c.mark()
        c.wait("gpsimd", "dve", mi["dve"])
        greg = es.enter_context(nc.gpsimd.register("greg"))
        greg2 = es.enter_context(nc.gpsimd.register("greg2"))

        def gather(q):
            q.reg_load(greg, self_i[0:1, 0:1])
            q.reg_mul(greg, greg, ND * 2048)
            q.dma_start(out=w1sel[:, :].rearrange("p (c o) -> p c o", c=8),
                        in_=bass.AP(W1t, greg,
                                    [[2048, 128], [128 * 2048, 8], [1, 2048]])
                        ).then_inc(c.sems["gpd"], 16)
            q.reg_load(greg2, self_i[0:1, 0:1])
            q.reg_mul(greg2, greg2, 2048 * ND)
            q.dma_start(out=w2sel[:, :].rearrange("p (f o) -> p f o", f=NF),
                        in_=bass.AP(W2t, greg2,
                                    [[ND, 128], [128 * ND, NF], [1, ND]])
                        ).then_inc(c.sems["gpd"], 16)
            q.reg_load(greg, self_i[0:1, 0:1])
            q.reg_mul(greg, greg, 2048)
            q.dma_start(out=b1c[:, :],
                        in_=bass.AP(b1t, greg, [[1, 128], [128, NF]]),
                        allow_slow_non_contiguous=True
                        ).then_inc(c.sems["gpd"], 16)
            q.reg_load(greg2, self_i[0:1, 0:1])
            q.reg_mul(greg2, greg2, ND)
            q.dma_start(out=b2c[:, :],
                        in_=bass.AP(b2t, greg2, [[1, 128], [128, NC8]]),
                        allow_slow_non_contiguous=True
                        ).then_inc(c.sems["gpd"], 16)
        c.on("gpsimd", gather)
        c.cnt["gpd"] += 16 * 4
        for ch in range(NC8):
            c.dma("gpsimd", xn2_fm[:, ch * Tq:(ch + 1) * Tq],
                  xn2T[128 * ch:128 * (ch + 1), :])
        barrier_all(c, ENGINES)

        for fc in range(NF):
            for tb in range(NTB):
                pw = (fc * NTB + tb) % 2
                c.wait("tensor", "act", c.cnt["act"])

                def mm1(q, fc=fc, tb=tb, pw=pw):
                    for ch in range(NC8):
                        r = q.matmul(ps_a[pw][:, 0:512],
                                     w1sel[:, ch * 2048 + fc * 128:
                                           ch * 2048 + (fc + 1) * 128],
                                     xn2_fm[:, ch * Tq + tb * 512:
                                            ch * Tq + (tb + 1) * 512],
                                     start=(ch == 0), stop=(ch == NC8 - 1))
                    return r
                c.inc("tensor", "pe", mm1)
                c.wait("scalar", "pe", c.cnt["pe"])
                c.inc("scalar", "act",
                      lambda q, fc=fc, tb=tb, pw=pw:
                      q.activation(h_fm[:, fc * Tq + tb * 512:
                                        fc * Tq + (tb + 1) * 512],
                                   ps_a[pw][:, 0:512], AF.Gelu,
                                   bias=b1c[:, fc:fc + 1]))
        barrier_all(c, ("tensor", "scalar"))

        for oc in range(NC8):
            for tb in range(NTB):
                pw = (oc * NTB + tb) % 2
                c.wait("tensor", "act", c.cnt["act"])

                def mm2(q, oc=oc, tb=tb, pw=pw):
                    for fc in range(NF):
                        r = q.matmul(ps_a[pw][:, 0:512],
                                     w2sel[:, fc * ND + oc * 128:
                                           fc * ND + (oc + 1) * 128],
                                     h_fm[:, fc * Tq + tb * 512:
                                          fc * Tq + (tb + 1) * 512],
                                     start=(fc == 0), stop=(fc == NF - 1))
                    return r
                c.inc("tensor", "pe", mm2)
                c.wait("scalar", "pe", c.cnt["pe"])
                c.inc("scalar", "act",
                      lambda q, oc=oc, tb=tb, pw=pw:
                      q.activation(r_fm[:, oc * Tq + tb * 512:
                                        oc * Tq + (tb + 1) * 512],
                                   ps_a[pw][:, 0:512], AF.Identity,
                                   bias=b2c[:, oc:oc + 1]))
        barrier_all(c, ("tensor", "scalar", "vector"))

        for tt in range(Tq // 128):
            mW = c.mark()
            c.wait("sync", "dve", mW["dve"])
            c.dma("sync", x1t[:, :], x1[tt * 128:(tt + 1) * 128, :])
            c.dma("sync", ya2t[:, :], ya2[tt * 128:(tt + 1) * 128, :])
            c.dma("sync", xn2t[:, :], xn2r[tt * 128:(tt + 1) * 128, :])
            c.wait("tensor", "dve", c.cnt["dve"])
            for ob in range(2):
                pw = ob % 2

                def mmo(q, tt=tt, ob=ob, pw=pw):
                    for ch in range(NC8):
                        q.matmul(ps_a[pw][:, 0:512],
                                 r_fm[:, ch * Tq + tt * 128:ch * Tq + (tt + 1) * 128],
                                 wo_sb[:, ch * ND + ob * 512:ch * ND + (ob + 1) * 512],
                                 start=(ch == 0), stop=False)
                    return q.matmul(ps_a[pw][:, 0:512], ones_r[0:1, :],
                                    borow[0:1, ob * 512:(ob + 1) * 512],
                                    start=False, stop=True)
                c.inc("tensor", "pe", mmo)
            mo = c.mark()
            c.wait("vector", "pe", mo["pe"])
            c.wait("vector", "syd", mo["syd"])
            for ob in range(2):
                sl = slice(ob * 512, (ob + 1) * 512)
                c.sinc("vector", "dve",
                       lambda q, ob=ob, sl=sl:
                       q.scalar_tensor_tensor(out32[:, sl], ps_a[ob][:, 0:512],
                                              0.3, x1t[:, sl], ALU.mult, ALU.add))
                c.sinc("vector", "dve",
                       lambda q, sl=sl:
                       q.tensor_tensor(out32[:, sl], out32[:, sl], ya2t[:, sl],
                                       ALU.add))
                c.sinc("vector", "dve",
                       lambda q, sl=sl:
                       q.tensor_tensor(out32[:, sl], out32[:, sl], xn2t[:, sl],
                                       ALU.add))
            mo2 = c.mark()
            c.wait("sync", "dve", mo2["dve"])
            c.dma("sync", x2[tt * 128:(tt + 1) * 128, :], out32[:, :])
        mf = c.mark()
        c.wait("sync", "syd", mf["syd"])
        c.emit()
    return nc


DIM_HEAD = 64
WINDOW = 512
HALO = 640


def rotary_tables(g0, R):
    half_idx = np.arange(0, DIM_HEAD, 2, dtype=np.float64)
    inv_freq = 1.0 / 10000.0 ** (half_idx / DIM_HEAD)
    t = (np.arange(R, dtype=np.float64) + g0)
    freqs = t[:, None] * inv_freq[None, :]
    freqs = np.concatenate([freqs, freqs], axis=-1)
    base = (half_idx + 0.4 * DIM_HEAD) / (1.4 * DIM_HEAD)
    power = (t - NT // 2) / float(WINDOW // 2)
    scale = base[None, :] ** power[:, None]
    scale = np.concatenate([scale, scale], axis=-1)
    cos, sin = np.cos(freqs), np.sin(freqs)
    def fm(a):
        return np.ascontiguousarray(np.tile(a.T, (2, 1)).astype(np.float32))
    return (fm(cos * scale), fm(sin * scale),
            fm(cos / scale), fm(sin / scale))


def gain_cols(g):
    col = np.tile(np.asarray(g, np.float32), 2)[:, None]
    sw = np.concatenate([g[32:], g[:32]])
    swc = np.tile(np.asarray(sw, np.float32), 2)[:, None]
    return col, swc


def kvalid_tiles(g0):
    out = np.zeros((128, KPT + 5), dtype=np.float32)
    for ktp in range(KPT + 5):
        gl = g0 + (ktp - KPT) * 128 + np.arange(128)
        out[:, ktp] = ((ktp >= KPT) & (gl >= 0)).astype(np.float32)
    return out


def _split_rows(arr):
    out = []
    for cid in range(NCORES):
        b, h = cid // 2, cid % 2
        out.append(np.ascontiguousarray(arr[b, HALF * h:HALF * (h + 1)]))
    return out


def _cols128(v, n):
    out = np.zeros((128, n), dtype=np.float32)
    v = np.asarray(v, dtype=np.float32)
    for i in range(n):
        seg = v[i * 128:(i + 1) * 128]
        out[:len(seg), i] = seg
    return out


_ONESAT = np.concatenate(
    [np.concatenate([np.ones((1, 64)), np.zeros((1, 64))], 1),
     np.concatenate([np.zeros((1, 64)), np.ones((1, 64))], 1)], 0
).astype(np.float32)

RFF = 2688


def _run(nc, in_maps):
    res = run_bass_kernel_spmd(nc, in_maps, core_ids=list(range(NCORES)))
    return res.results


_BUILD_CACHE = {}


def _get(name, fn):
    if name not in _BUILD_CACHE:
        _BUILD_CACHE[name] = fn()
    return _BUILD_CACHE[name]


_DBG = {}


def kernel(**inputs):
    f32 = np.float32
    inp = {k: np.asarray(v, dtype=f32) for k, v in inputs.items()}
    x = inp["L0_attn_x"]

    nc = _get("ln", lambda: build_ln(HALF))
    res = _run(nc, [{"x": s} for s in _split_rows(x)])
    xn = np.stack([np.concatenate([res[2 * b]["xn"], res[2 * b + 1]["xn"]])
                   for b in range(NB)])

    nc = _get("l0attn", lambda: build_l0_attn(HALF, NT, HALF))
    im = []
    for cid in range(NCORES):
        b, h = cid // 2, cid % 2
        im.append({"xnT": np.ascontiguousarray(xn[b].T),
                   "qnT": np.ascontiguousarray(xn[b, HALF * h:HALF * (h + 1)].T),
                   "wat": inp["L0_attn_w_attn"], "bat": inp["L0_attn_b_attn"],
                   "wpr": inp["L0_attn_w_proj"], "bpr": inp["L0_attn_b_proj"],
                   "qoff": np.full((128, 1), HALF * h, dtype=f32)})
    res = _run(nc, im)
    ya = np.stack([np.concatenate([res[2 * b]["ya"], res[2 * b + 1]["ya"]])
                   for b in range(NB)])

    nc_a = _get("pffa", lambda: build_pffn_attn(RFF))
    nc_f = _get("pfff", lambda: build_pffn_ffn(RFF))
    t_cur = xn
    for l in range(2):
        qs = inp["L0_ffn_q_scale"][l]
        ks = inp["L0_ffn_k_scale"][l]
        qc, qw = gain_cols(qs)
        kc, kw = gain_cols(ks)
        im = []
        for cid in range(NCORES):
            b, h = cid // 2, cid % 2
            g0 = HALF * h - HALO
            if g0 < 0:
                sl = np.concatenate([np.zeros((-g0, ND), dtype=f32),
                                     t_cur[b][:g0 + RFF]])
            else:
                sl = t_cur[b][g0:g0 + RFF]
            tabs = rotary_tables(g0, RFF)
            im.append({"tT": np.ascontiguousarray(sl.T),
                       "wqkv": inp["L0_ffn_wqkv"][l], "wo": inp["L0_ffn_wo"][l],
                       "qscl": np.concatenate([qc, kc], 1),
                       "qssw": np.concatenate([qw, kw], 1),
                       "tabcq": tabs[0], "tabsq": tabs[1],
                       "tabck": tabs[2], "tabsk": tabs[3],
                       "kvalid": kvalid_tiles(g0), "onesat": _ONESAT})
        res = _run(nc_a, im)
        t2 = np.stack([np.concatenate([res[2 * b]["t2T"].T[HALO:HALO + HALF],
                                       res[2 * b + 1]["t2T"].T[HALO:HALO + HALF]])
                       for b in range(NB)])
        im = []
        for cid in range(NCORES):
            b, h = cid // 2, cid % 2
            g0 = HALF * h - HALO
            if g0 < 0:
                sl = np.concatenate([np.zeros((-g0, ND), dtype=f32),
                                     t2[b][:g0 + RFF]])
            else:
                sl = t2[b][g0:g0 + RFF]
            im.append({"t2T": np.ascontiguousarray(sl.T),
                       "lngc": _cols128(inp["L0_ffn_ln_g"][l], 8),
                       "lnbc": _cols128(inp["L0_ffn_ln_b"][l], 8),
                       "w1": inp["L0_ffn_w1"][l],
                       "b1a": _cols128(inp["L0_ffn_b1"][l][:HID], NFS),
                       "b1g": _cols128(inp["L0_ffn_b1"][l][HID:], NFS),
                       "w2": inp["L0_ffn_w2"][l], "b2": inp["L0_ffn_b2"][l]})
        res = _run(nc_f, im)
        t_cur = np.stack([np.concatenate([res[2 * b]["t3T"].T[HALO:HALO + HALF],
                                          res[2 * b + 1]["t3T"].T[HALO:HALO + HALF]])
                          for b in range(NB)])
    yf = t_cur
    _DBG.update(xn=xn, ya=ya, yf=yf)

    nc = _get("asmln", lambda: build_assemble_ln(HALF))
    xs, yas, yfs = _split_rows(x), _split_rows(ya), _split_rows(yf)
    res = _run(nc, [{"x": xs[i], "ya": yas[i], "yf": yfs[i]}
                    for i in range(NCORES)])
    x1 = np.stack([np.concatenate([res[2 * b]["x1"], res[2 * b + 1]["x1"]])
                   for b in range(NB)])
    xn2 = np.stack([np.concatenate([res[2 * b]["xn2"], res[2 * b + 1]["xn2"]])
                    for b in range(NB)])

    _DBG.update(x1=x1, xn2=xn2)
    nc = _get("l1attn", lambda: build_l1_attn(HALF, NT))
    im = []
    for cid in range(NCORES):
        b, h = cid // 2, cid % 2
        im.append({"xnT": np.ascontiguousarray(xn2[b].T),
                   "qnT": np.ascontiguousarray(xn2[b, HALF * h:HALF * (h + 1)].T),
                   "qW": inp["L1_attn_qW"], "qb": inp["L1_attn_qb"],
                   "kW": inp["L1_attn_kW"], "kb": inp["L1_attn_kb"],
                   "vW": inp["L1_attn_vW"], "vb": inp["L1_attn_vb"],
                   "oW": inp["L1_attn_oW"], "ob_": inp["L1_attn_ob"],
                   "J": inp["L1_attn_J"]})
    res = _run(nc, im)
    ya2 = np.stack([np.concatenate([res[2 * b]["ya2T"].T,
                                    res[2 * b + 1]["ya2T"].T])
                    for b in range(NB)])

    _DBG.update(ya2=ya2)
    nc = _get("l1ffn", lambda: build_l1_ffn(HALF))
    im = []
    for cid in range(NCORES):
        b, h = cid // 2, cid % 2
        sl = slice(HALF * h, HALF * (h + 1))
        im.append({"xn2T": np.ascontiguousarray(xn2[b, sl].T),
                   "x1": np.ascontiguousarray(x1[b, sl]),
                   "ya2": np.ascontiguousarray(ya2[b, sl]),
                   "xn2r": np.ascontiguousarray(xn2[b, sl]),
                   "last": np.ascontiguousarray(xn2[b, NT - 1:NT]),
                   "Wp1": inp["L1_ffn_Wp1"], "bp1": inp["L1_ffn_bp1"],
                   "Wp2": inp["L1_ffn_Wp2"], "bp2": inp["L1_ffn_bp2"],
                   "W1": inp["L1_ffn_W1"], "b1": inp["L1_ffn_b1"],
                   "W2": inp["L1_ffn_W2"], "b2": inp["L1_ffn_b2"],
                   "Wo": inp["L1_ffn_Wo"], "bo": inp["L1_ffn_bo"]})
    res = _run(nc, im)
    x2 = np.stack([np.concatenate([res[2 * b]["x2"], res[2 * b + 1]["x2"]])
                   for b in range(NB)])
    return x2
